# version 25
# baseline (speedup 1.0000x reference)
import numpy as np
from contextlib import ExitStack
import concourse.bass as bass
import concourse.mybir as mybir
from concourse.bass_utils import run_bass_kernel_spmd

F32 = mybir.dt.float32
BF16 = mybir.dt.bfloat16
AX = mybir.AxisListType.X
AF = mybir.ActivationFunctionType
ALU = mybir.AluOpType

D = 2048
H = 16
NOPE, ROPE, VD = 128, 64, 128
QHD = NOPE + ROPE
QL = 1536
LAT = 512
KVA = LAT + ROPE
INTER = 5632
SCALE = float(QHD) ** -0.5
EPS_RMS = 1e-6
EPS_LN = 1e-5
DC = D // 128
QLC = QL // 128
LATC = LAT // 128
IC = INTER // 128
NEG = -1.0e30


class Sem:
    def __init__(self, es, nc, name):
        self.h = es.enter_context(nc.semaphore(name))
        self.n = 0


class Prog:

    def __init__(self, nc, es):
        self.nc = nc
        self.es = es
        self.streams = {k: [] for k in ("tensor", "vector", "scalar", "sync", "gpsimd")}

    def sem(self, name):
        return Sem(self.es, self.nc, name)

    def op(self, eng, fn, inc=None, k=1):
        if inc is not None:
            inc.n += k
            h = inc.h

            def wrapped(e, fn=fn, h=h, kk=k):
                fn(e).then_inc(h, kk)

            self.streams[eng].append(wrapped)
        else:
            self.streams[eng].append(lambda e, fn=fn: fn(e))

    def w(self, eng, sem, n=None):
        v = sem.n if n is None else n
        if v > 0:
            self.streams[eng].append(lambda e, h=sem.h, v=v: e.wait_ge(h, v))

    def wf(self, eng, free):
        if free is not None:
            self.w(eng, free[0], free[1])

    def emit(self):
        with self.nc.Block(no_gpsimd_drain=True) as blk:
            @blk.tensor
            def _(t):
                for fn in self.streams["tensor"]:
                    fn(t)

            @blk.vector
            def _(v):
                for fn in self.streams["vector"]:
                    fn(v)

            @blk.scalar
            def _(s):
                for fn in self.streams["scalar"]:
                    fn(s)

            @blk.sync
            def _(s):
                for fn in self.streams["sync"]:
                    fn(s)

            @blk.gpsimd
            def _(g):
                for fn in self.streams["gpsimd"]:
                    fn(g)


def build_program(S=2048, dbg=()):
    assert S % 512 == 0
    TC = S // 128
    NGRP = S // 512
    NP = S // 512
    nc = bass.Bass("TRN2", target_bir_lowering=False)

    def din(name, shape):
        return nc.dram_tensor(name, list(shape), F32, kind="ExternalInput").ap()

    x_hbm = din("x", [S, D])
    qa_wT = din("qa_wT", [D, QL])
    kva_wT = din("kva_wT", [D, KVA])
    qbn_g = din("qbn_g", [4, QLC, 128, 512])
    qbp_g = din("qbp_g", [2, QLC, 128, 512])
    kvbk_l = din("kvbk_l", [H, 128, LAT])
    kvbv_l = din("kvbv_l", [H, 128, LAT])
    ow_l = din("ow_l", [DC, 128, D])
    gate_l = din("gate_l", [IC, 128, D])
    up_l = din("up_l", [IC, 128, D])
    down_c = din("down_c", [DC, 128, INTER])
    qaln_cols_in = din("qaln_cols", [128, QLC])
    kvln_cols_in = din("kvln_cols", [128, LATC])
    gates_c = din("gates_c", [128, IC])
    ups_c = din("ups_c", [128, IC])
    downs_c = din("downs_c", [128, DC])
    ident_in = din("ident", [128, 128])
    mask_in = din("mask", [128, 128])
    rotT_in = din("rotT", [128, 128])
    cos_tm_in = din("cos_tm", [S, ROPE])
    sins_tm_in = din("sins_tm", [S, ROPE])
    cos_fm2_in = din("cos_fm2", [128, S])
    sin_fm2_in = din("sin_fm2", [128, S])

    out_hbm = nc.dram_tensor("out", [S, D], F32, kind="ExternalOutput").ap()
    dbg = set(dbg)

    es = ExitStack()
    with es:
        P = Prog(nc, es)

        def sb(name, shape, dt=BF16, side=None):
            kw = {"side": side} if side else {}
            return es.enter_context(nc.sbuf_tensor(name, list(shape), dt, **kw))

        dWs = [P.sem("dW0"), P.sem("dW1")]
        dGs = [P.sem("dG0"), P.sem("dG1")]
        dSh = P.sem("dSh")
        dC0 = P.sem("dC0")
        dC1 = P.sem("dC1")
        dC2 = P.sem("dC2")
        dCF = P.sem("dCF")
        dOut = P.sem("dOut")
        dDump = P.sem("dDump")
        cW = P.sem("cW")
        cS = P.sem("cS")
        tMM = P.sem("tMM")
        tTR = P.sem("tTR")
        vE = P.sem("vE")
        vStat = P.sem("vStat")
        sAct = P.sem("sAct")

        def dump(name, tile, cols, dt, when_sem, when_n):
            if name not in dbg:
                return
            o = nc.dram_tensor("d_" + name, [128, cols], dt, kind="ExternalOutput").ap()
            P.w("gpsimd", when_sem, when_n)
            P.op("gpsimd", lambda e, t=tile, o=o, c=cols:
                 e.dma_start(out=o, in_=t[:, 0:c]), inc=dDump, k=16)

        cI = sb("cI", [128, 128])
        cMask = sb("cMask", [128, 128])
        stage = [sb(f"stage{i}", [128, 512], F32) for i in range(2)]
        stats = sb("stats", [128, 22], F32)

        stage_free = [None, None]
        stage_i = [0]

        def stage_load(src_ap, cols):
            assert cols <= 512
            i = stage_i[0] % 2
            stage_i[0] += 1
            P.wf("sync", stage_free[i])
            P.op("sync", lambda e, s=src_ap, i=i, c=cols:
                 e.dma_start(out=stage[i][:, 0:c], in_=s), inc=dWs[i], k=16)
            return i, dWs[i].n

        def cast_stage(slot, ld, dst_ap, cols, eng="vector", scale=None):
            P.w(eng, dWs[slot], ld)
            if eng == "scalar":
                if scale is None:
                    P.op(eng, lambda e, i=slot, d=dst_ap, c=cols:
                         e.copy(d, stage[i][:, 0:c]), inc=cS)
                else:
                    P.op(eng, lambda e, i=slot, d=dst_ap, c=cols, s=scale:
                         e.activation(d, stage[i][:, 0:c], AF.Copy, scale=s), inc=cS)
                stage_free[slot] = (cS, cS.n)
                return cS.n
            if scale is None:
                P.op(eng, lambda e, i=slot, d=dst_ap, c=cols:
                     e.tensor_copy(d, stage[i][:, 0:c]), inc=cW)
            else:
                P.op(eng, lambda e, i=slot, d=dst_ap, c=cols, s=scale:
                     e.tensor_scalar_mul(d, stage[i][:, 0:c], s), inc=cW)
            stage_free[slot] = (cW, cW.n)
            return cW.n

        def load_cast(src_ap, dst_tile, dst_c0, cols, eng="vector", scale=None):
            done = 0
            c0 = 0
            while c0 < cols:
                cc = min(512, cols - c0)
                sl, ld = stage_load(src_ap[:, c0:c0 + cc], cc)
                done = cast_stage(sl, ld, dst_tile[:, dst_c0 + c0:dst_c0 + c0 + cc],
                                  cc, eng, scale)
                c0 += cc
            return done

        def mk_path(sb_fn, name, eng, cast_eng, nslots=2, slot_cols=1024):
            tiles = [sb_fn(f"stg_{name}{i}", [128, slot_cols], F32) for i in range(nslots)]
            sems = [P.sem(f"d_{name}{i}") for i in range(nslots)]
            cP = P.sem(f"c_{name}")
            st = {"free": [None] * nslots, "i": 0}

            def lc(src_ap, dst_tile, dst_c0, cols, scale=None):
                done = None
                c0 = 0
                while c0 < cols:
                    cc = min(slot_cols, cols - c0)
                    i = st["i"] % nslots
                    st["i"] += 1
                    P.wf(eng, st["free"][i])
                    P.op(eng, lambda e, s=src_ap[:, c0:c0 + cc], i=i, cc=cc:
                         e.dma_start(out=tiles[i][:, 0:cc], in_=s), inc=sems[i], k=16)
                    ld = sems[i].n
                    P.w(cast_eng, sems[i], ld)
                    if cast_eng == "scalar":
                        if scale is None:
                            P.op(cast_eng, lambda e, i=i, d=dst_tile, a=dst_c0 + c0, cc=cc:
                                 e.copy(d[:, a:a + cc], tiles[i][:, 0:cc]), inc=cP)
                        else:
                            P.op(cast_eng, lambda e, i=i, d=dst_tile, a=dst_c0 + c0,
                                 cc=cc, sc=scale:
                                 e.activation(d[:, a:a + cc], tiles[i][:, 0:cc],
                                              AF.Copy, scale=sc), inc=cP)
                    else:
                        if scale is None:
                            P.op(cast_eng, lambda e, i=i, d=dst_tile, a=dst_c0 + c0, cc=cc:
                                 e.tensor_copy(d[:, a:a + cc], tiles[i][:, 0:cc]), inc=cP)
                        else:
                            P.op(cast_eng, lambda e, i=i, d=dst_tile, a=dst_c0 + c0,
                                 cc=cc, sc=scale:
                                 e.tensor_scalar_mul(d[:, a:a + cc],
                                                     tiles[i][:, 0:cc], sc), inc=cP)
                    st["free"][i] = (cP, cP.n)
                    done = (cP, cP.n)
                    c0 += cc
                return done
            return lc

        def gate():
            for eng in ("tensor", "vector", "scalar", "sync", "gpsimd"):
                for s in (tMM, tTR, vE, vStat, sAct, cW, cS, dDump, dSh):
                    P.w(eng, s, s.n)

        load_cast(ident_in, cI, 0, 128)
        load_cast(mask_in, cMask, 0, 128)
        consts_done = cW.n

        qln_c = sb("qln_c", [128, QLC], F32)
        kln_c = sb("kln_c", [128, LATC], F32)
        P.op("sync", lambda e: e.dma_start(out=qln_c[:], in_=qaln_cols_in), inc=dC0, k=16)
        P.op("sync", lambda e: e.dma_start(out=kln_c[:], in_=kvln_cols_in), inc=dC0, k=16)
        ln_cols_ld = dC0.n

        scl = sb("scl", [128, 1], F32)
        rql = sb("rql", [128, 1], F32)
        P.op("vector", lambda e: e.memset(scl[:], float(QHD)), inc=vStat)
        P.op("vector", lambda e: e.memset(rql[:], float(QL)), inc=vStat)
        P.w("vector", vStat, vStat.n)
        P.op("vector", lambda e: e.reciprocal(scl[:], scl[:]), inc=vStat)
        P.op("vector", lambda e: e.reciprocal(rql[:], rql[:]), inc=vStat)
        P.w("scalar", vStat, vStat.n)
        P.op("scalar", lambda e: e.activation(scl[:], scl[:], AF.Sqrt), inc=sAct)
        scl_done = sAct.n

        rqStack = ExitStack()
        oxStack = ExitStack()
        OX = None

        a1_done_v = q_done_v = heads_done_v = kpe_hi_done = 0
        with ExitStack() as phA:
            def sbA(name, shape, dt=BF16):
                return phA.enter_context(nc.sbuf_tensor(name, list(shape), dt))

            c_fm = sbA("c_fm", [128, LATC * S])
            kpe_lo = sbA("kpe_lo", [128, S])
            rmsqa_fm = rqStack.enter_context(
                nc.sbuf_tensor("rmsqa_fm", [128, QLC * S], BF16, side="right"))

            P.op("vector", lambda e: e.memset(kpe_lo[64:128, :], 0.0), inc=vE)
            kpe_zeroed = vE.n

            with ExitStack() as ph1:
                def sb1(name, shape, dt=BF16):
                    return ph1.enter_context(nc.sbuf_tensor(name, list(shape), dt))

                def ps1(name, shape):
                    return ph1.enter_context(nc.psum_tensor(name, list(shape), F32))

                qa_w_bf = sb1("qa_w_bf", [128, DC * QL])
                kva_w_bf = sb1("kva_w_bf", [128, DC * KVA])
                cos_tm = sb1("cos_tm_sb", [128, TC * ROPE], F32)
                sins_tm = sb1("sins_tm_sb", [128, TC * ROPE], F32)
                x_tm_bf = [sb1(f"x_tm_bf{i}", [128, D]) for i in range(2)]
                x0fm_t = [sb1(f"x0fm_t{i}", [128, D]) for i in range(2)]
                rq_tm = [sb1(f"rq_tm{i}", [128, QL]) for i in range(2)]
                ckv_tm = [sb1(f"ckv_tm{i}", [128, LAT]) for i in range(2)]
                kpe_sc = sb1("kpe_sc", [128, ROPE], F32)
                kpe_r1 = sb1("kpe_r1", [128, ROPE], F32)
                kpe_r2 = sb1("kpe_r2", [128, ROPE], F32)
                kpe_bf = sb1("kpe_bf", [128, ROPE])
                junk = sb1("junk", [128, 2048])

                pQA = ps1("pQA", [128, QL])
                pCKV = ps1("pCKV", [128, LAT])
                pKPE = ps1("pKPE", [128, 64])
                pTR = ps1("pTR", [128, 512])
                pTR2 = ps1("pTR2", [128, 512])

                for t in range(TC):
                    P.op("sync", lambda e, t=t: e.dma_start(
                        out=cos_tm[:, t * ROPE:(t + 1) * ROPE],
                        in_=cos_tm_in[t * 128:(t + 1) * 128, :]), inc=dC1, k=16)
                    P.op("sync", lambda e, t=t: e.dma_start(
                        out=sins_tm[:, t * ROPE:(t + 1) * ROPE],
                        in_=sins_tm_in[t * 128:(t + 1) * 128, :]), inc=dC1, k=16)
                tm_ld = dC1.n

                for d in range(DC):
                    load_cast(qa_wT[d * 128:(d + 1) * 128, :], qa_w_bf, d * QL, QL)
                for d in range(DC):
                    load_cast(kva_wT[d * 128:(d + 1) * 128, :], kva_w_bf, d * KVA, KVA)
                a1_w = cW.n

                x_free = [None, None]
                epi_free = [None, None]
                x0_free = [None, None]
                pqa_free = None
                pckv_free = None
                pkpe_free = None
                ptr_free = None

                x_cast_t = [None] * (TC + 1)
                P.wf("vector", epi_free[0])
                x_cast_t[0] = load_cast(x_hbm[0:128, :], x_tm_bf[0], 0, D)
                ptrb = [pTR, pTR2]
                ptr_free2 = [None, None]
                gctr = 0
                pend_epi = None

                def emit_a1_epi(pi, pt, pqa_mm, kpe_mm, ckv_mm):
                    nonlocal pqa_free, pckv_free, pkpe_free
                    P.w("scalar", sAct, sAct.n)
                    P.w("scalar", tMM, pqa_mm)
                    P.op("scalar", lambda e: e.activation(
                        junk[:, 0:QL], pQA[:], AF.Square,
                        accum_out=stats[:, 0:1]), inc=sAct)
                    sq_qa = sAct.n
                    P.w("vector", dC1, tm_ld)
                    P.w("vector", tMM, kpe_mm)
                    P.op("vector", lambda e: e.tensor_copy(kpe_sc[:], pKPE[:]), inc=vE)
                    pkpe_free = (vE, vE.n)
                    P.w("vector", vE, vE.n)
                    P.op("vector", lambda e, pt=pt: e.tensor_mul(
                        kpe_r1[:], kpe_sc[:], cos_tm[:, pt * ROPE:(pt + 1) * ROPE]), inc=vE)
                    swap_ap = bass.AP(kpe_sc, 32, [[ROPE, 128], [-32, 2], [1, 32]])
                    P.op("vector", lambda e, pt=pt, sw=swap_ap: e.tensor_tensor(
                        kpe_r2[:].rearrange("p (a b) -> p a b", a=2), sw,
                        sins_tm[:, pt * ROPE:(pt + 1) * ROPE].rearrange("p (a b) -> p a b", a=2),
                        ALU.mult), inc=vE)
                    P.w("vector", vE, vE.n)
                    P.op("vector", lambda e: e.tensor_add(
                        kpe_bf[:], kpe_r1[:], kpe_r2[:]), inc=vE)
                    kpe_done = vE.n
                    P.w("vector", sAct, sq_qa)
                    P.op("vector", lambda e: e.tensor_scalar(
                        stats[:, 1:2], stats[:, 0:1], rql[:], EPS_RMS,
                        op0=ALU.mult, op1=ALU.add), inc=vStat)
                    P.w("vector", vStat, vStat.n)
                    P.op("vector", lambda e: e.reciprocal(stats[:, 1:2], stats[:, 1:2]), inc=vStat)
                    r_qa = vStat.n
                    P.w("scalar", vStat, r_qa)
                    P.op("scalar", lambda e: e.activation(
                        stats[:, 2:3], stats[:, 1:2], AF.Sqrt), inc=sAct)
                    P.w("scalar", sAct, sAct.n)
                    P.op("scalar", lambda e, pi=pi: e.activation(
                        rq_tm[pi][:], pQA[:], AF.Copy,
                        scale=stats[:, 2:3]), inc=sAct)
                    pqa_free = (sAct, sAct.n)
                    P.w("scalar", tMM, ckv_mm)
                    P.op("scalar", lambda e: e.activation(
                        junk[:, 1536:1536 + LAT], pCKV[:], AF.Square,
                        accum_out=stats[:, 3:4]), inc=sAct)
                    sq_ckv = sAct.n
                    P.w("vector", sAct, sq_ckv)
                    P.op("vector", lambda e: e.tensor_scalar(
                        stats[:, 4:5], stats[:, 3:4], 1.0 / LAT, EPS_RMS,
                        op0=ALU.mult, op1=ALU.add), inc=vStat)
                    P.w("vector", vStat, vStat.n)
                    P.op("vector", lambda e: e.reciprocal(stats[:, 4:5], stats[:, 4:5]), inc=vStat)
                    r_ckv = vStat.n
                    P.w("scalar", vStat, r_ckv)
                    P.op("scalar", lambda e: e.activation(
                        stats[:, 5:6], stats[:, 4:5], AF.Sqrt), inc=sAct)
                    P.w("scalar", sAct, sAct.n)
                    P.op("scalar", lambda e, pi=pi: e.activation(
                        ckv_tm[pi][:], pCKV[:], AF.Copy,
                        scale=stats[:, 5:6]), inc=sAct)
                    pckv_free = (sAct, sAct.n)
                    return kpe_done

                def emit_a1_tr(pi, pt, pkpe_done):
                    nonlocal gctr
                    P.w("tensor", vE, pkpe_done)
                    P.wf("tensor", pqa_free)
                    P.wf("tensor", pckv_free)
                    for k4 in range(3):
                        b = gctr % 2
                        gctr += 1
                        P.wf("tensor", ptr_free2[b])
                        for kk in range(4):
                            k = k4 * 4 + kk
                            P.op("tensor", lambda e, pi=pi, k=k, kk=kk, b=b: e.matmul(
                                ptrb[b][:, kk * 128:(kk + 1) * 128],
                                rq_tm[pi][:, k * 128:(k + 1) * 128], cI[:],
                                start=True, stop=True), inc=(tTR if kk == 3 else None))
                        P.w("vector", tTR, tTR.n)
                        oa = bass.AP(rmsqa_fm, (k4 * 4) * S + pt * 128,
                                     [[QLC * S, 128], [S, 4], [1, 128]])
                        P.op("vector", lambda e, oa=oa, b=b: e.tensor_copy(
                            oa, ptrb[b][:].rearrange("p (a b) -> p a b", a=4)), inc=vE)
                        ptr_free2[b] = (vE, vE.n)
                    b = gctr % 2
                    gctr += 1
                    P.wf("tensor", ptr_free2[b])
                    for kk in range(4):
                        P.op("tensor", lambda e, pi=pi, kk=kk, b=b: e.matmul(
                            ptrb[b][:, kk * 128:(kk + 1) * 128],
                            ckv_tm[pi][:, kk * 128:(kk + 1) * 128], cI[:],
                            start=True, stop=True), inc=(tTR if kk == 3 else None))
                    P.w("vector", tTR, tTR.n)
                    oa = bass.AP(c_fm, pt * 128, [[LATC * S, 128], [S, 4], [1, 128]])
                    P.op("vector", lambda e, oa=oa, b=b: e.tensor_copy(
                        oa, ptrb[b][:].rearrange("p (a b) -> p a b", a=4)), inc=vE)
                    ptr_free2[b] = (vE, vE.n)
                    b = gctr % 2
                    gctr += 1
                    P.wf("tensor", ptr_free2[b])
                    P.op("tensor", lambda e, b=b: e.matmul(
                        ptrb[b][0:64, 0:128], kpe_bf[:], cI[:],
                        start=True, stop=True), inc=tTR)
                    P.w("vector", tTR, tTR.n)
                    P.w("vector", vE, kpe_zeroed)
                    P.op("vector", lambda e, pt=pt, b=b: e.tensor_copy(
                        kpe_lo[0:64, pt * 128:(pt + 1) * 128], ptrb[b][0:64, 0:128]), inc=vE)
                    ptr_free2[b] = (vE, vE.n)

                for t in range(TC):
                    i = t % 2
                    x_cast = x_cast_t[t]
                    P.w("tensor", cW, max(x_cast, consts_done))
                    for d4 in range(4):
                        b = gctr % 2
                        gctr += 1
                        P.wf("tensor", ptr_free2[b])
                        for dd in range(4):
                            d = d4 * 4 + dd
                            P.op("tensor", lambda e, i=i, d=d, dd=dd, b=b: e.matmul(
                                ptrb[b][:, dd * 128:(dd + 1) * 128],
                                x_tm_bf[i][:, d * 128:(d + 1) * 128], cI[:],
                                start=True, stop=True), inc=(tTR if dd == 3 else None))
                        P.w("vector", tTR, tTR.n)
                        P.wf("vector", x0_free[i])
                        P.op("vector", lambda e, i=i, d4=d4, b=b: e.tensor_copy(
                            x0fm_t[i][:, d4 * 512:(d4 + 1) * 512], ptrb[b][:]), inc=vE)
                        ptr_free2[b] = (vE, vE.n)
                    x_fm_done = vE.n
                    epi_free[i] = (tTR, tTR.n)
                    if t + 1 < TC:
                        ni = (t + 1) % 2
                        P.wf("vector", epi_free[ni])
                        x_cast_t[t + 1] = load_cast(
                            x_hbm[(t + 1) * 128:(t + 2) * 128, :], x_tm_bf[ni], 0, D)
                    if pend_epi is not None:
                        emit_a1_tr(*pend_epi)
                    P.w("tensor", cW, a1_w)
                    P.w("tensor", vE, x_fm_done)
                    P.wf("tensor", pqa_free)
                    P.wf("tensor", pckv_free)
                    P.wf("tensor", pkpe_free)
                    for d in range(DC):
                        st, sp = (d == 0), (d == DC - 1)
                        for j in range(3):
                            P.op("tensor", lambda e, i=i, d=d, j=j, st=st, sp=sp: e.matmul(
                                pQA[:, j * 512:(j + 1) * 512],
                                x0fm_t[i][:, d * 128:(d + 1) * 128],
                                qa_w_bf[:, d * QL + j * 512:d * QL + (j + 1) * 512],
                                start=st, stop=sp),
                                inc=(tMM if (sp and j == 2) else None))
                    pqa_mm = tMM.n
                    for d in range(DC):
                        st, sp = (d == 0), (d == DC - 1)
                        P.op("tensor", lambda e, i=i, d=d, st=st, sp=sp: e.matmul(
                            pKPE[:], x0fm_t[i][:, d * 128:(d + 1) * 128],
                            kva_w_bf[:, d * KVA + LAT:d * KVA + KVA],
                            start=st, stop=sp), inc=(tMM if sp else None))
                    kpe_mm = tMM.n
                    for d in range(DC):
                        st, sp = (d == 0), (d == DC - 1)
                        P.op("tensor", lambda e, i=i, d=d, st=st, sp=sp: e.matmul(
                            pCKV[:], x0fm_t[i][:, d * 128:(d + 1) * 128],
                            kva_w_bf[:, d * KVA:d * KVA + LAT],
                            start=st, stop=sp), inc=(tMM if sp else None))
                    a1_mm = tMM.n
                    x0_free[i] = (tMM, a1_mm)
                    kd = emit_a1_epi(i, t, pqa_mm, kpe_mm, a1_mm)
                    pend_epi = (i, t, kd)

                emit_a1_tr(*pend_epi)
                a1_done_v = vE.n
                kpe_hi_done = 0

            dump("rmsqa", rmsqa_fm, QLC * S, BF16, vE, a1_done_v)
            dump("cfm", c_fm, LATC * S, BF16, vE, a1_done_v)
            dump("kpelo", kpe_lo, S, BF16, vE, a1_done_v)

            Q_all = sbA("Q_all", [128, 24 * S])
            gate()
            with ExitStack() as ph2:
                def sb2(name, shape, dt=BF16):
                    return ph2.enter_context(nc.sbuf_tensor(name, list(shape), dt))

                def ps2(name, shape):
                    return ph2.enter_context(nc.psum_tensor(name, list(shape), F32))

                qbw_bf = sb2("qbw_bf", [128, 18 * 512])
                cRotT = sb2("cRotT", [128, 128])
                cosf = sb2("cosf", [128, S])
                sinf = sb2("sinf", [128, S])
                pe_raw = sb2("pe_raw", [128, 512])
                t1 = sb2("t1", [128, 512], F32)
                t2 = sb2("t2", [128, 512], F32)
                t3 = sb2("t3", [128, 512], F32)

                pW = [ps2("pW0", [128, 512]), ps2("pW1", [128, 512])]
                pROT = ps2("pROT", [128, 512])
                lc_a2 = mk_path(sb2, "a2", "scalar", "scalar",
                                nslots=2, slot_cols=512)

                load_cast(cos_fm2_in, cosf, 0, S)
                load_cast(sin_fm2_in, sinf, 0, S)
                load_cast(rotT_in, cRotT, 0, 128)
                fm2_ld = cW.n

                pw_free = [None, None]
                prot_free = None
                qbw_free = None
                pe_raw_free = [None]
                t1_free = [None]
                t2_free = [None]

                slot_free = [None] * 18
                gidx = [0]

                def q_group(src, grp, chunks, is_pe):
                    nonlocal prot_free, qbw_free
                    g = gidx[0]
                    gidx[0] += 1
                    slots = [(g * QLC + k) % 18 for k in range(QLC)]
                    w_cp = None
                    for k in range(QLC):
                        if k % 2 == 0:
                            P.w("vector", dC0, ln_cols_ld)
                            P.wf("vector", slot_free[slots[k]])
                            load_cast(src[grp, k], qbw_bf, slots[k] * 512, 512,
                                      scale=qln_c[:, k:k + 1])
                        else:
                            P.w("scalar", dC0, ln_cols_ld)
                            P.wf("scalar", slot_free[slots[k]])
                            w_cp = lc_a2(src[grp, k], qbw_bf, slots[k] * 512, 512,
                                         scale=qln_c[:, k:k + 1])
                    w_ready = cW.n
                    for cc, ch in enumerate(chunks):
                        for piece in range(NP):
                            pb = (cc * NP + piece) % 2
                            P.w("tensor", cW, w_ready)
                            P.wf("tensor", w_cp)
                            P.w("tensor", vE, a1_done_v)
                            P.wf("tensor", pw_free[pb])
                            for k in range(QLC):
                                P.op("tensor", lambda e, pb=pb, sk=slots[k], k=k, cc=cc, piece=piece: e.matmul(
                                    pW[pb][:],
                                    qbw_bf[:, sk * 512 + cc * 128:sk * 512 + (cc + 1) * 128],
                                    rmsqa_fm[:, k * S + piece * 512:k * S + (piece + 1) * 512],
                                    start=(k == 0), stop=(k == QLC - 1)),
                                    inc=(tMM if k == QLC - 1 else None))
                            mm = tMM.n
                            P.w("vector", tMM, mm)
                            if not is_pe:
                                P.w("vector", sAct, scl_done)
                                P.op("vector", lambda e, pb=pb, ch=ch, piece=piece: e.tensor_scalar_mul(
                                    Q_all[:, ch * S + piece * 512:ch * S + (piece + 1) * 512],
                                    pW[pb][:], scl[:]), inc=vE)
                                pw_free[pb] = (vE, vE.n)
                            else:
                                P.wf("vector", pe_raw_free[0])
                                P.op("vector", lambda e, pb=pb: e.tensor_copy(
                                    pe_raw[:], pW[pb][:]), inc=vE)
                                P.w("vector", cW, fm2_ld)
                                P.wf("vector", t1_free[0])
                                P.op("vector", lambda e, pb=pb, piece=piece: e.tensor_mul(
                                    t1[:], pW[pb][:],
                                    cosf[:, piece * 512:(piece + 1) * 512]), inc=vE)
                                pw_free[pb] = (vE, vE.n)
                                pe_ev = vE.n
                                P.w("tensor", vE, pe_ev)
                                P.w("tensor", cW, fm2_ld)
                                P.wf("tensor", prot_free)
                                P.op("tensor", lambda e: e.matmul(
                                    pROT[:], cRotT[:], pe_raw[:],
                                    start=True, stop=True), inc=tMM)
                                pe_raw_free[0] = (tMM, tMM.n)
                                P.w("vector", tMM, tMM.n)
                                P.wf("vector", t2_free[0])
                                P.op("vector", lambda e, piece=piece: e.tensor_mul(
                                    t2[:], pROT[:],
                                    sinf[:, piece * 512:(piece + 1) * 512]), inc=vE)
                                prot_free = (vE, vE.n)
                                P.w("vector", vE, vE.n)
                                P.w("vector", sAct, scl_done)
                                P.op("vector", lambda e: e.tensor_add(t3[:], t1[:], t2[:]), inc=vE)
                                t1_free[0] = (vE, vE.n)
                                t2_free[0] = (vE, vE.n)
                                P.w("vector", vE, vE.n)
                                P.op("vector", lambda e, ch=ch, piece=piece: e.tensor_scalar_mul(
                                    Q_all[:, ch * S + piece * 512:ch * S + (piece + 1) * 512],
                                    t3[:], scl[:]), inc=vE)
                    for sk in slots:
                        slot_free[sk] = (tMM, tMM.n)
                    qbw_free = (tMM, tMM.n)

                for grp in range(4):
                    q_group(qbn_g, grp, [4 * grp + j for j in range(4)], False)
                for grp in range(2):
                    q_group(qbp_g, grp, [16 + 4 * grp + j for j in range(4)], True)
                q_done_v = vE.n
            rqStack.close()

            dump("qall", Q_all, 24 * S, BF16, vE, q_done_v)

            OX = oxStack.enter_context(
                nc.sbuf_tensor("OX", [128, DC * S], BF16, side="right"))
            gate()
            with ExitStack() as ph3:
                def sb3(name, shape, dt=BF16):
                    return ph3.enter_context(nc.sbuf_tensor(name, list(shape), dt))

                def ps3(name, shape):
                    return ph3.enter_context(nc.psum_tensor(name, list(shape), F32))

                kfm = sb3("kfm", [128, S])
                vtm = sb3("vtm", [128, S])
                kvw3 = [sb3(f"kvw3_{i}", [128, LAT]) for i in range(3)]
                Ebuf = [sb3(f"Ebuf{i}", [128, S]) for i in range(2)]
                PTsb = [sb3(f"PTsb{i}", [128, 512]) for i in range(2)]
                Dg = [sb3(f"Dg{i}", [128, 128]) for i in range(2)]

                pPT = [ps3("pPT0", [128, 512]), ps3("pPT1", [128, 512])]
                pS = ps3("pS", [128, S])
                pX = ps3("pX", [128, 512])
                pO = ps3("pO", [128, 512])
                ring = [(pS, r * 512) for r in range(S // 512)] + [(pX, 0)]
                NR = len(ring)
                pW = pPT

                kv_free = None
                ppt_free = [None, None]
                pw_free = ppt_free
                e_free = [None, None]
                dg_free = [None, None]
                region_free = [None] * (S // 512 + 1)
                rp = [0]
                po_free = None
                pts_free = [None, None]

                def kv_load(src, dst):
                    sl, ld = stage_load(src, LAT)
                    P.w("vector", dWs[sl], ld)
                    P.w("vector", dC0, ln_cols_ld)
                    for l in range(LATC):
                        P.op("vector", lambda e, sl=sl, dst=dst, l=l: e.tensor_scalar_mul(
                            dst[:, l * 128:(l + 1) * 128],
                            stage[sl][:, l * 128:(l + 1) * 128],
                            kln_c[:, l:l + 1]), inc=cW)
                    stage_free[sl] = (cW, cW.n)
                    return cW.n

                kvw_ready = {}
                kvw_ready[(0, 0)] = kv_load(kvbk_l[0], kvw3[0])
                kvw_ready[(0, 1)] = kv_load(kvbv_l[0], kvw3[1])
                vmm_prev = [0]

                def emit_kprod(h):
                    kw_sb = kvw3[(2 * h) % 3]
                    P.w("tensor", cW, kvw_ready[(h, 0)])
                    P.w("tensor", vE, a1_done_v)
                    kpr = [None] * NP
                    for piece in range(NP):
                        pb = piece % 2
                        P.wf("tensor", pw_free[pb])
                        for l in range(LATC):
                            P.op("tensor", lambda e, pb=pb, l=l, piece=piece, w=kw_sb: e.matmul(
                                pW[pb][:],
                                w[:, l * 128:(l + 1) * 128],
                                c_fm[:, l * S + piece * 512:l * S + (piece + 1) * 512],
                                start=(l == 0), stop=(l == LATC - 1)),
                                inc=(tMM if l == LATC - 1 else None))
                        P.w("vector", tMM, tMM.n)
                        P.op("vector", lambda e, pb=pb, piece=piece: e.tensor_copy(
                            kfm[:, piece * 512:(piece + 1) * 512], pW[pb][:]), inc=vE)
                        pw_free[pb] = (vE, vE.n)
                        kpr[piece] = (vE, vE.n)
                    kmm = tMM.n
                    if h + 1 < H:
                        P.w("vector", tMM, vmm_prev[0])
                        kvw_ready[(h + 1, 0)] = kv_load(
                            kvbk_l[h + 1], kvw3[(2 * h + 2) % 3])
                    return kpr, kmm

                def emit_vprod(h, kmm):
                    vw_sb = kvw3[(2 * h + 1) % 3]
                    P.w("tensor", cW, kvw_ready[(h, 1)])
                    for tch in range(TC):
                        pb = tch % 2
                        P.wf("tensor", pw_free[pb])
                        for l in range(LATC):
                            P.op("tensor", lambda e, pb=pb, l=l, tch=tch, w=vw_sb: e.matmul(
                                pW[pb][:, 0:128],
                                c_fm[:, l * S + tch * 128:l * S + (tch + 1) * 128],
                                w[:, l * 128:(l + 1) * 128],
                                start=(l == 0), stop=(l == LATC - 1)),
                                inc=(tMM if l == LATC - 1 else None))
                        P.w("vector", tMM, tMM.n)
                        P.op("vector", lambda e, pb=pb, tch=tch: e.tensor_copy(
                            vtm[:, tch * 128:(tch + 1) * 128], pW[pb][:, 0:128]), inc=vE)
                        pw_free[pb] = (vE, vE.n)
                    vmm_prev[0] = tMM.n
                    if h + 1 < H:
                        P.w("vector", tMM, kmm)
                        kvw_ready[(h + 1, 1)] = kv_load(
                            kvbv_l[h + 1], kvw3[(2 * h) % 3])
                    return (vE, vE.n)

                kpr_n, kmm_n = emit_kprod(0)
                kvr_n = emit_vprod(0, kmm_n)

                kpe_shift_done = 0
                for h in range(H):
                    kpr, kv_ready = kpr_n, kvr_n
                    if h == 8:
                        P.w("gpsimd", tMM, tMM.n)
                        P.op("gpsimd", lambda e: e.dma_start(
                            out=kpe_lo[64:128, :], in_=kpe_lo[0:64, :]), inc=dSh, k=16)
                        P.w("vector", dSh, dSh.n)
                        P.op("vector", lambda e: e.memset(kpe_lo[0:64, :], 0.0), inc=vE)
                        kpe_shift_done = vE.n

                    kpe_t = kpe_lo
                    pch = 16 + h % 8

                    pendq = []
                    for qc in range(TC + 2):
                        if qc < TC:
                            ei = qc % 2
                            kcols = (qc + 1) * 128
                            ngr = (kcols + 511) // 512
                            par = 11 * (qc % 2)
                            regs = [(rp[0] + g) % NR for g in range(ngr)]
                            rp[0] = (rp[0] + ngr) % NR
                            P.w("tensor", vE, q_done_v)
                            P.w("tensor", vE, max(kpe_shift_done, kpe_zeroed))
                            gmm = []
                            for g in range(ngr):
                                n = min(512, kcols - g * 512)
                                rt, off = ring[regs[g]]
                                P.wf("tensor", kpr[g])
                                P.wf("tensor", region_free[regs[g]])
                                P.op("tensor", lambda e, h=h, qc=qc, g=g, n=n, rt=rt, off=off: e.matmul(
                                    rt[:, off:off + n],
                                    Q_all[:, h * S + qc * 128:h * S + (qc + 1) * 128],
                                    kfm[:, g * 512:g * 512 + n],
                                    start=True, stop=False))
                                if g == ngr - 1:
                                    moff = off + (qc % 4) * 128
                                    P.op("tensor", lambda e, rt=rt, moff=moff: e.matmul(
                                        rt[:, moff:moff + 128],
                                        cI[:], cMask[:], start=False, stop=False))
                                P.op("tensor", lambda e, qc=qc, g=g, n=n, pch=pch,
                                     kp=kpe_t, rt=rt, off=off: e.matmul(
                                    rt[:, off:off + n],
                                    Q_all[:, pch * S + qc * 128:pch * S + (qc + 1) * 128],
                                    kp[:, g * 512:g * 512 + n],
                                    start=False, stop=True), inc=tMM)
                                gmm.append(tMM.n)
                                P.w("vector", tMM, gmm[g])
                                P.op("vector", lambda e, g=g, n=n, rt=rt, off=off, par=par: e.reduce_max(
                                    stats[:, par + g:par + g + 1], rt[:, off:off + n],
                                    axis=AX), inc=vStat)
                        if len(pendq) >= 2 or (qc >= TC and pendq):
                            pqc, pdg, pexp = pendq.pop(0)
                            pei = pqc % 2
                            nkt = pqc + 1
                            first_pv = True
                            pvq = None

                            def emit_pv(sl, b0, bn, ev):
                                nonlocal first_pv
                                P.w("tensor", sAct, ev)
                                if first_pv:
                                    P.wf("tensor", kv_ready)
                                    P.wf("tensor", po_free)
                                    first_pv = False
                                for bb in range(bn):
                                    kt = b0 + bb
                                    P.op("tensor", lambda e, kt=kt, bb=bb, sl=sl, nkt=nkt: e.matmul(
                                        pO[:, 0:128],
                                        vtm[:, kt * 128:(kt + 1) * 128],
                                        PTsb[sl][:, bb * 128:(bb + 1) * 128],
                                        start=(kt == 0), stop=(kt == nkt - 1)),
                                        inc=(tMM if kt == nkt - 1 else None))
                                pts_free[sl] = (tMM, tMM.n)

                            for b0 in range(0, nkt, 4):
                                bn = min(4, nkt - b0)
                                sl = (b0 // 4) % 2
                                P.w("tensor", vStat, pdg)
                                P.w("tensor", sAct, pexp)
                                P.wf("tensor", ppt_free[sl])
                                for bb in range(bn):
                                    kt = b0 + bb
                                    P.op("tensor", lambda e, pei=pei, kt=kt, bb=bb, sl=sl: e.matmul(
                                        pPT[sl][:, bb * 128:(bb + 1) * 128],
                                        Ebuf[pei][:, kt * 128:(kt + 1) * 128], Dg[pei][:],
                                        start=True, stop=True),
                                        inc=(tTR if bb == bn - 1 else None))
                                P.w("scalar", tTR, tTR.n)
                                P.wf("scalar", pts_free[sl])
                                P.op("scalar", lambda e, bn=bn, sl=sl: e.copy(
                                    PTsb[sl][:, 0:bn * 128], pPT[sl][:, 0:bn * 128]), inc=sAct)
                                ppt_free[sl] = (sAct, sAct.n)
                                if pvq is not None:
                                    emit_pv(*pvq)
                                pvq = (sl, b0, bn, sAct.n)
                            emit_pv(*pvq)
                            o_mm = tMM.n
                            e_free[pei] = (tTR, tTR.n)
                            dg_free[pei] = (tTR, tTR.n)
                            P.w("scalar", tMM, o_mm)
                            P.op("scalar", lambda e, h=h, pqc=pqc: e.copy(
                                OX[:, h * S + pqc * 128:h * S + (pqc + 1) * 128],
                                pO[:, 0:128]), inc=sAct)
                            po_free = (sAct, sAct.n)
                            if pqc == TC - 2 and h + 1 < H:
                                kpr_n, kmm_n = emit_kprod(h + 1)
                        if qc < TC:
                            P.w("vector", vStat, vStat.n)
                            P.op("vector", lambda e, ngr=ngr, par=par: e.reduce_max(
                                stats[:, par + 4:par + 5], stats[:, par:par + ngr],
                                axis=AX, negate=True), inc=vStat)
                            neg_done = vStat.n
                            P.wf("scalar", e_free[ei])
                            P.w("scalar", vStat, neg_done)
                            for g in range(ngr):
                                n = min(512, kcols - g * 512)
                                rt, off = ring[regs[g]]
                                P.op("scalar", lambda e, ei=ei, g=g, n=n, rt=rt, off=off, par=par: e.activation(
                                    Ebuf[ei][:, g * 512:g * 512 + n], rt[:, off:off + n],
                                    AF.Exp, bias=stats[:, par + 4:par + 5],
                                    accum_out=stats[:, par + 5 + g:par + 6 + g]), inc=sAct)
                                region_free[regs[g]] = (sAct, sAct.n)
                            exp_done = sAct.n
                            P.w("vector", sAct, exp_done)
                            P.op("vector", lambda e, ngr=ngr, par=par: e.reduce_sum(
                                stats[:, par + 9:par + 10], stats[:, par + 5:par + 5 + ngr],
                                axis=AX), inc=vStat)
                            P.w("vector", vStat, vStat.n)
                            P.op("vector", lambda e, par=par: e.reciprocal(
                                stats[:, par + 10:par + 11], stats[:, par + 9:par + 10]), inc=vStat)
                            P.w("vector", vStat, vStat.n)
                            P.wf("vector", dg_free[ei])
                            P.op("vector", lambda e, ei=ei, par=par: e.tensor_scalar_mul(
                                Dg[ei][:], cI[:], stats[:, par + 10:par + 11]), inc=vStat)
                            pendq.append((qc, vStat.n, exp_done))
                    if h + 1 < H:
                        kvr_n = emit_vprod(h + 1, kmm_n)

                heads_done_s = sAct.n

            dump("ofm", OX, DC * S, BF16, sAct, heads_done_s)

        a4_done_v = 0
        gate()
        with ExitStack() as ph4:
            def sb4(name, shape, dt=BF16):
                return ph4.enter_context(nc.sbuf_tensor(name, list(shape), dt))

            def ps4(name, shape):
                return ph4.enter_context(nc.psum_tensor(name, list(shape), F32))

            ow_bf = sb4("ow_bf", [128, DC * D])
            x2 = sb4("x2", [128, D], F32)
            x1tm = [sb4(f"x1tm{i}", [128, D]) for i in range(2)]
            xst4 = [sb4(f"xst4_{i}", [128, D], F32) for i in range(2)]
            junk4 = sb4("junk4", [128, D])

            pOP = ps4("pOP", [128, D])
            pTR4 = ps4("pTR4", [128, 512])
            lc_a4 = mk_path(sb4, "a4", "scalar", "scalar",
                            nslots=2, slot_cols=1024)

            x4_free = [None, None]
            x_ld4 = [None] * TC
            for t in range(2):
                P.op("gpsimd", lambda e, t=t, i=t % 2: e.dma_start(
                    out=xst4[i][:], in_=x_hbm[t * 128:(t + 1) * 128, :]),
                    inc=dGs[t % 2], k=16)
                x_ld4[t] = dGs[t % 2].n

            ow_cast = []
            for d in range(DC):
                if d % 2 == 0:
                    ow_cast.append((cW, load_cast(ow_l[d], ow_bf, d * D, D)))
                else:
                    ow_cast.append(lc_a4(ow_l[d], ow_bf, d * D, D))

            pop_free = None
            ptr4_free = None
            x1tm_free = [None, None]
            x2_free = None
            pend_tr = None

            def emit_x1_transposes(pt, pi, papply):
                nonlocal ptr4_free
                P.w("tensor", vE, papply)
                for d4 in range(4):
                    P.wf("tensor", ptr4_free)
                    for dd in range(4):
                        d = d4 * 4 + dd
                        P.op("tensor", lambda e, d=d, dd=dd, pi=pi: e.matmul(
                            pTR4[:, dd * 128:(dd + 1) * 128],
                            x1tm[pi][:, d * 128:(d + 1) * 128], cI[:],
                            start=True, stop=True), inc=(tTR if dd == 3 else None))
                    P.w("vector", tTR, tTR.n)
                    oa = bass.AP(OX, (d4 * 4) * S + pt * 128,
                                 [[DC * S, 128], [S, 4], [1, 128]])
                    P.op("vector", lambda e, oa=oa: e.tensor_copy(
                        oa, pTR4[:].rearrange("p (a b) -> p a b", a=4)), inc=vE)
                    ptr4_free = (vE, vE.n)
                x1tm_free[pi] = (tTR, tTR.n)

            for t in range(TC):
                i = t % 2
                x_ld = x_ld4[t]
                P.w("tensor", sAct, heads_done_s)
                P.wf("tensor", pop_free)
                for hv in range(DC):
                    if t == 0:
                        P.wf("tensor", ow_cast[hv])
                    st, sp = (hv == 0), (hv == DC - 1)
                    for j in range(4):
                        P.op("tensor", lambda e, hv=hv, t=t, j=j, st=st, sp=sp: e.matmul(
                            pOP[:, j * 512:(j + 1) * 512],
                            OX[:, hv * S + t * 128:hv * S + (t + 1) * 128],
                            ow_bf[:, hv * D + j * 512:hv * D + (j + 1) * 512],
                            start=st, stop=sp),
                            inc=(tMM if (sp and j == 3) else None))
                op_mm = tMM.n
                P.w("vector", tMM, op_mm)
                P.w("vector", dGs[i], x_ld)
                P.wf("vector", x2_free)
                P.op("vector", lambda e, i=i: e.tensor_add(x2[:], pOP[:], xst4[i][:]), inc=vE)
                x2_done = vE.n
                pop_free = (vE, x2_done)
                x4_free[i] = (vE, x2_done)
                if t + 2 < TC:
                    P.wf("gpsimd", x4_free[i])
                    P.op("gpsimd", lambda e, t2=t + 2, i=i: e.dma_start(
                        out=xst4[i][:], in_=x_hbm[t2 * 128:(t2 + 1) * 128, :]),
                        inc=dGs[i], k=16)
                    x_ld4[t + 2] = dGs[i].n
                P.w("vector", vE, x2_done)
                P.op("vector", lambda e: e.reduce_sum(stats[:, 10:11], x2[:], axis=AX), inc=vStat)
                P.w("vector", vStat, vStat.n)
                P.op("vector", lambda e: e.tensor_scalar_mul(
                    stats[:, 11:12], stats[:, 10:11], -1.0 / D), inc=vStat)
                P.op("vector", lambda e: e.tensor_scalar_mul(
                    stats[:, 14:15], stats[:, 10:11], 1.0 / D), inc=vStat)
                nm_done = vStat.n
                P.w("scalar", sAct, sAct.n)
                P.w("scalar", vStat, nm_done)
                P.op("scalar", lambda e: e.activation(
                    junk4[:], x2[:], AF.Square, bias=stats[:, 11:12],
                    accum_out=stats[:, 12:13]), inc=sAct)
                P.w("vector", sAct, sAct.n)
                P.op("vector", lambda e: e.tensor_scalar(
                    stats[:, 13:14], stats[:, 12:13], 1.0 / D, EPS_LN,
                    op0=ALU.mult, op1=ALU.add), inc=vStat)
                P.w("vector", vStat, vStat.n)
                P.op("vector", lambda e: e.reciprocal(stats[:, 13:14], stats[:, 13:14]), inc=vStat)
                P.w("scalar", vStat, vStat.n)
                P.op("scalar", lambda e: e.activation(
                    stats[:, 13:14], stats[:, 13:14], AF.Sqrt), inc=sAct)
                P.w("vector", sAct, sAct.n)
                P.wf("vector", x1tm_free[i])
                P.op("vector", lambda e, i=i: e.tensor_scalar(
                    x1tm[i][:], x2[:], stats[:, 14:15], stats[:, 13:14],
                    op0=ALU.subtract, op1=ALU.mult), inc=vE)
                x1tm_done = vE.n
                x2_free = (vE, x1tm_done)
                if pend_tr is not None:
                    emit_x1_transposes(*pend_tr)
                pend_tr = (t, i, x1tm_done)
            emit_x1_transposes(*pend_tr)

            a4_done_v = vE.n

        dump("x1fm", OX, DC * S, BF16, vE, a4_done_v)

        gate()
        with ExitStack() as phF:
            def sbF(name, shape, dt=BF16):
                return phF.enter_context(nc.sbuf_tensor(name, list(shape), dt))

            def psF(name, shape):
                return phF.enter_context(nc.psum_tensor(name, list(shape), F32))

            h_fm = sbF("h_fm", [128, IC * 512])
            gbf = [sbF(f"gbf{i}", [128, D]) for i in range(2)]
            ubf = [sbF(f"ubf{i}", [128, D]) for i in range(2)]
            dbf = [sbF(f"dbf{i}", [128, INTER // 2]) for i in range(2)]
            sig = sbF("sig", [128, 512])
            tmid = sbF("tmid", [128, 512], F32)
            x2f = [sbF(f"x2f{i}", [128, 512], F32) for i in range(2)]
            x2tm = sbF("x2tm", [128, 4 * D], F32)
            outst = [sbF("outst0", [128, D], F32)] * 2
            gates_sb = sbF("gates_sb", [128, IC], F32)
            cIf = sbF("cIf", [128, 128], F32)
            ups_sb = sbF("ups_sb", [128, IC], F32)
            downs_sb = sbF("downs_sb", [128, DC], F32)

            pG = [psF("pG0", [128, 512]), psF("pG1", [128, 512])]
            pU = [psF("pU0", [128, 512]), psF("pU1", [128, 512])]
            pD = [psF("pD0", [128, 512]), psF("pD1", [128, 512])]
            pT2 = psF("pT2", [128, 512])
            lc_gp = mk_path(sbF, "gp", "gpsimd", "vector")
            lc_sy = mk_path(sbF, "sy2", "sync", "scalar", nslots=3)

            sl, ld = stage_load(ident_in[:, 0:128], 128)
            P.w("vector", dWs[sl], ld)
            P.op("vector", lambda e, i=sl: e.tensor_copy(cIf[:], stage[i][:, 0:128]), inc=cW)
            stage_free[sl] = (cW, cW.n)
            cif_ld = cW.n
            P.op("sync", lambda e: e.dma_start(out=gates_sb[:], in_=gates_c), inc=dCF, k=16)
            P.op("sync", lambda e: e.dma_start(out=ups_sb[:], in_=ups_c), inc=dCF, k=16)
            P.op("sync", lambda e: e.dma_start(out=downs_sb[:], in_=downs_c), inc=dCF, k=16)
            scales_ld = dCF.n

            gbf_free = [None, None]
            ubf_free = [None, None]
            dbf_free = [None, None]
            pg_free = [None, None]
            pu_free = [None, None]
            pd_free = [None, None]
            pt2_free = None
            x2f_free = [None, None]
            h_free = None
            sig_free = tmid_free = None
            x2tm_free = None
            out_free = [None, None]
            HALF = IC // 2

            for grp in range(NGRP):
                gc0 = grp * 512
                def gu_loads(ii):
                    wi = ii % 2
                    P.wf("scalar", gbf_free[wi])
                    g_ready = lc_sy(gate_l[ii], gbf[wi], 0, D)
                    P.wf("vector", ubf_free[wi])
                    u_ready = lc_gp(up_l[ii], ubf[wi], 0, D)
                    return g_ready, u_ready

                pre = gu_loads(0)
                P.w("tensor", vE, a4_done_v)
                P.wf("tensor", h_free)
                for ii in range(IC):
                    wi = ii % 2
                    g_ready, u_ready = pre
                    P.wf("tensor", g_ready)
                    P.wf("tensor", pg_free[wi])
                    for d in range(DC):
                        P.op("tensor", lambda e, wi=wi, d=d, gc0=gc0: e.matmul(
                            pG[wi][:], gbf[wi][:, d * 128:(d + 1) * 128],
                            OX[:, d * S + gc0:d * S + gc0 + 512],
                            start=(d == 0), stop=(d == DC - 1)),
                            inc=(tMM if d == DC - 1 else None))
                    g_mm = tMM.n
                    P.wf("tensor", u_ready)
                    P.wf("tensor", pu_free[wi])
                    for d in range(DC):
                        P.op("tensor", lambda e, wi=wi, d=d, gc0=gc0: e.matmul(
                            pU[wi][:], ubf[wi][:, d * 128:(d + 1) * 128],
                            OX[:, d * S + gc0:d * S + gc0 + 512],
                            start=(d == 0), stop=(d == DC - 1)),
                            inc=(tMM if d == DC - 1 else None))
                    u_mm = tMM.n
                    gbf_free[wi] = (tMM, g_mm)
                    ubf_free[wi] = (tMM, u_mm)
                    if ii + 1 < IC:
                        pre = gu_loads(ii + 1)
                    P.w("scalar", tMM, g_mm)
                    P.w("scalar", dCF, scales_ld)
                    P.wf("scalar", sig_free)
                    P.op("scalar", lambda e, wi=wi, ii=ii: e.activation(
                        sig[:], pG[wi][:], AF.Sigmoid, scale=gates_sb[:, ii:ii + 1]), inc=sAct)
                    sig_done = sAct.n
                    P.w("vector", sAct, sig_done)
                    P.wf("vector", tmid_free)
                    P.op("vector", lambda e, wi=wi, ii=ii: e.scalar_tensor_tensor(
                        tmid[:], pG[wi][:], gates_sb[:, ii:ii + 1], sig[:],
                        op0=ALU.mult, op1=ALU.mult), inc=vE)
                    tm_done = vE.n
                    pg_free[wi] = (vE, tm_done)
                    sig_free = (vE, tm_done)
                    P.w("vector", tMM, u_mm)
                    P.w("vector", vE, tm_done)
                    P.op("vector", lambda e, wi=wi, ii=ii: e.scalar_tensor_tensor(
                        h_fm[:, ii * 512:(ii + 1) * 512], pU[wi][:], ups_sb[:, ii:ii + 1], tmid[:],
                        op0=ALU.mult, op1=ALU.mult), inc=vE)
                    h_done = vE.n
                    pu_free[wi] = (vE, h_done)
                    tmid_free = (vE, h_done)
                h_ready = vE.n
                def down_loads(do):
                    P.wf("scalar", dbf_free[0])
                    h0 = lc_sy(down_c[do][:, 0:INTER // 2], dbf[0], 0, INTER // 2)
                    P.wf("vector", dbf_free[1])
                    h1 = lc_gp(down_c[do][:, INTER // 2:INTER], dbf[1], 0, INTER // 2)
                    return [h0, h1]

                pre_d = down_loads(0)
                pend_tr2 = None

                def emit_down_tr(pdo, pxi, pdone):
                    nonlocal pt2_free, x2f_free
                    P.w("tensor", cW, cif_ld)
                    P.w("tensor", vE, pdone)
                    P.wf("tensor", pt2_free)
                    for tc in range(4):
                        P.op("tensor", lambda e, tc=tc, pxi=pxi: e.matmul(
                            pT2[:, tc * 128:(tc + 1) * 128],
                            x2f[pxi][:, tc * 128:(tc + 1) * 128], cIf[:],
                            start=True, stop=True), inc=(tTR if tc == 3 else None))
                    x2f_free[pxi] = (tTR, tTR.n)
                    P.w("vector", tTR, tTR.n)
                    if pdo == 0:
                        P.wf("vector", x2tm_free)
                    oa = bass.AP(x2tm, pdo * 128, [[4 * D, 128], [D, 4], [1, 128]])
                    P.op("vector", lambda e, oa=oa: e.tensor_copy(
                        oa, pT2[:].rearrange("p (a b) -> p a b", a=4)), inc=vE)
                    pt2_free = (vE, vE.n)

                for do in range(DC):
                    di = do % 2
                    half_ready = pre_d
                    P.wf("tensor", pd_free[di])
                    for hh in range(2):
                        P.wf("tensor", half_ready[hh])
                        P.w("tensor", vE, h_ready)
                        for jj in range(HALF):
                            ii = hh * HALF + jj
                            P.op("tensor", lambda e, di=di, hh=hh, jj=jj, ii=ii: e.matmul(
                                pD[di][:], dbf[hh][:, jj * 128:(jj + 1) * 128],
                                h_fm[:, ii * 512:(ii + 1) * 512],
                                start=(ii == 0), stop=(ii == IC - 1)),
                                inc=(tMM if jj == HALF - 1 else None))
                        dbf_free[hh] = (tMM, tMM.n)
                    d_mm = tMM.n
                    if do + 1 < DC:
                        pre_d = down_loads(do + 1)
                    P.w("vector", tMM, d_mm)
                    P.w("vector", dCF, scales_ld)
                    P.wf("vector", x2f_free[di])
                    P.op("vector", lambda e, di=di, do=do, gc0=gc0: e.scalar_tensor_tensor(
                        x2f[di][:], pD[di][:], downs_sb[:, do:do + 1],
                        OX[:, do * S + gc0:do * S + gc0 + 512],
                        op0=ALU.mult, op1=ALU.add), inc=vE)
                    x2f_done = vE.n
                    pd_free[di] = (vE, x2f_done)
                    if pend_tr2 is not None:
                        emit_down_tr(*pend_tr2)
                    pend_tr2 = (do, di, x2f_done)
                emit_down_tr(*pend_tr2)
                x2tm_ready = vE.n
                h_free = (tMM, d_mm)
                for tc in range(4):
                    t_abs = grp * 4 + tc
                    P.w("vector", vE, x2tm_ready)
                    P.op("vector", lambda e, tc=tc: e.reduce_sum(
                        stats[:, 10:11], x2tm[:, tc * D:(tc + 1) * D], axis=AX), inc=vStat)
                    P.w("vector", vStat, vStat.n)
                    P.op("vector", lambda e: e.tensor_scalar_mul(
                        stats[:, 11:12], stats[:, 10:11], -1.0 / D), inc=vStat)
                    P.op("vector", lambda e: e.tensor_scalar_mul(
                        stats[:, 14:15], stats[:, 10:11], 1.0 / D), inc=vStat)
                    nm_done = vStat.n
                    P.w("scalar", sAct, sAct.n)
                    P.w("scalar", vStat, nm_done)
                    P.wf("scalar", out_free[tc % 2])
                    P.op("scalar", lambda e, tc=tc: e.activation(
                        outst[tc % 2][:], x2tm[:, tc * D:(tc + 1) * D], AF.Square,
                        bias=stats[:, 11:12], accum_out=stats[:, 12:13]), inc=sAct)
                    P.w("vector", sAct, sAct.n)
                    P.op("vector", lambda e: e.tensor_scalar(
                        stats[:, 13:14], stats[:, 12:13], 1.0 / D, EPS_LN,
                        op0=ALU.mult, op1=ALU.add), inc=vStat)
                    P.w("vector", vStat, vStat.n)
                    P.op("vector", lambda e: e.reciprocal(stats[:, 13:14], stats[:, 13:14]), inc=vStat)
                    P.w("scalar", vStat, vStat.n)
                    P.op("scalar", lambda e: e.activation(
                        stats[:, 13:14], stats[:, 13:14], AF.Sqrt), inc=sAct)
                    P.w("vector", sAct, sAct.n)
                    P.op("vector", lambda e, tc=tc: e.tensor_scalar(
                        outst[tc % 2][:], x2tm[:, tc * D:(tc + 1) * D],
                        stats[:, 14:15], stats[:, 13:14],
                        op0=ALU.subtract, op1=ALU.mult), inc=vE)
                    apply_done = vE.n
                    P.w("gpsimd", vE, apply_done)
                    P.op("gpsimd", lambda e, t_abs=t_abs, tc=tc: e.dma_start(
                        out=out_hbm[t_abs * 128:(t_abs + 1) * 128, :], in_=outst[tc % 2][:]),
                        inc=dOut, k=16)
                    out_free[0] = out_free[1] = (dOut, dOut.n)
                x2tm_free = (vE, vE.n)

            P.w("gpsimd", dOut, dOut.n)
            P.w("gpsimd", dDump, dDump.n)

        oxStack.close()
        P.emit()
    return nc


def _host_prep(inputs, S=2048):
    ac = np.ascontiguousarray
    qb = inputs["L0_attn_q_b_w"]
    kvb = inputs["L0_attn_kv_b_w"]
    gate_w = inputs["L0_ffn_gate_w"]
    up_w = inputs["L0_ffn_up_w"]
    down_w = inputs["L0_ffn_down_w"]
    o_w = inputs["L0_attn_o_w"]

    d = {}
    d["qa_wT"] = ac(inputs["L0_attn_q_a_w"].T)
    d["kva_wT"] = ac(inputs["L0_attn_kv_a_w"].T)
    nope_rows = np.concatenate([qb[h * QHD:h * QHD + NOPE] for h in range(H)], 0)
    pe_order = [h for p in range(H // 2) for h in (p, p + H // 2)]
    pe_rows = np.concatenate([qb[h * QHD + NOPE:(h + 1) * QHD] for h in pe_order], 0)
    d["qbn_g"] = ac(nope_rows.T.reshape(QLC, 128, 4, 512).transpose(2, 0, 1, 3))
    d["qbp_g"] = ac(pe_rows.T.reshape(QLC, 128, 2, 512).transpose(2, 0, 1, 3))
    kk = np.stack([kvb[h * 256:h * 256 + 128] for h in range(H)], 0)
    vv = np.stack([kvb[h * 256 + 128:(h + 1) * 256] for h in range(H)], 0)
    d["kvbk_l"] = ac(kk.transpose(0, 2, 1).reshape(H, LATC, 128, 128)
                     .transpose(0, 2, 1, 3).reshape(H, 128, LAT))
    d["kvbv_l"] = ac(vv.transpose(0, 2, 1).reshape(H, LATC, 128, 128)
                     .transpose(0, 2, 1, 3).reshape(H, 128, LAT))
    d["ow_l"] = ac(o_w.T.reshape(DC, 128, D))
    d["gate_l"] = ac(gate_w.reshape(IC, 128, DC, 128).transpose(0, 3, 2, 1)
                     .reshape(IC, 128, D))
    d["up_l"] = ac(up_w.reshape(IC, 128, DC, 128).transpose(0, 3, 2, 1)
                   .reshape(IC, 128, D))
    d["down_c"] = ac(down_w.reshape(DC, 128, IC, 128).transpose(0, 3, 2, 1)
                     .reshape(DC, 128, INTER))
    d["qaln_cols"] = ac(inputs["L0_attn_q_a_ln_w"].reshape(QLC, 128).T)
    d["kvln_cols"] = ac(inputs["L0_attn_kv_a_ln_w"].reshape(LATC, 128).T)
    d["gates_c"] = ac(inputs["L0_ffn_gate_s"].reshape(IC, 128).T)
    d["ups_c"] = ac(inputs["L0_ffn_up_s"].reshape(IC, 128).T)
    d["downs_c"] = ac(inputs["L0_ffn_down_s"].reshape(DC, 128).T)
    d["ident"] = np.eye(128, dtype=np.float32)
    tri = np.tril(np.ones((128, 128), dtype=bool))
    d["mask"] = np.where(tri, 0.0, NEG).astype(np.float32)
    r64 = np.zeros((64, 64), dtype=np.float32)
    for i in range(32):
        r64[i, i + 32] = -1.0
        r64[i + 32, i] = 1.0
    r2 = np.zeros((128, 128), dtype=np.float32)
    r2[0:64, 0:64] = r64
    r2[64:128, 64:128] = r64
    d["rotT"] = ac(r2.T)
    inv_freq = (np.float32(1.0) /
                (np.float32(10000.0) **
                 (np.arange(0, ROPE, 2, dtype=np.float32) / np.float32(ROPE))))
    tpos = np.arange(S, dtype=np.float32)
    freqs = np.outer(tpos, inv_freq).astype(np.float32)
    emb = np.concatenate([freqs, freqs], -1)
    cos_t = np.cos(emb, dtype=np.float32)
    sin_t = np.sin(emb, dtype=np.float32)
    d["cos_tm"] = ac(cos_t)
    d["sins_tm"] = ac(np.concatenate([-sin_t[:, :32], sin_t[:, 32:]], 1))
    d["cos_fm2"] = ac(np.vstack([cos_t.T, cos_t.T]))
    d["sin_fm2"] = ac(np.vstack([sin_t.T, sin_t.T]))
    for k in d:
        d[k] = d[k].astype(np.float32, copy=False)
    return d


def kernel(**inputs):
    x = np.asarray(inputs["L0_attn_hidden_states"])
    B, S, _ = x.shape
    shared = _host_prep(inputs, S=S)
    nc = build_program(S=S)
    in_maps = []
    for b in range(B):
        m = dict(shared)
        m["x"] = np.ascontiguousarray(x[b])
        in_maps.append(m)
    res = run_bass_kernel_spmd(nc, in_maps, core_ids=list(range(B)))
    return np.stack([np.asarray(res.results[b]["out"], dtype=np.float32)
                     for b in range(B)], 0)


# version 27 (final)
# speedup vs baseline: 1.0283x; 1.0283x over previous
import numpy as np
from contextlib import ExitStack
import concourse.bass as bass
import concourse.mybir as mybir
from concourse.bass_utils import run_bass_kernel_spmd

F32 = mybir.dt.float32
BF16 = mybir.dt.bfloat16
AX = mybir.AxisListType.X
AF = mybir.ActivationFunctionType
ALU = mybir.AluOpType

D = 2048
H = 16
NOPE, ROPE, VD = 128, 64, 128
QHD = NOPE + ROPE
QL = 1536
LAT = 512
KVA = LAT + ROPE
INTER = 5632
SCALE = float(QHD) ** -0.5
EPS_RMS = 1e-6
EPS_LN = 1e-5
DC = D // 128
QLC = QL // 128
LATC = LAT // 128
IC = INTER // 128
NEG = -1.0e30


class Sem:
    def __init__(self, es, nc, name):
        self.h = es.enter_context(nc.semaphore(name))
        self.n = 0


class Prog:

    def __init__(self, nc, es):
        self.nc = nc
        self.es = es
        self.streams = {k: [] for k in ("tensor", "vector", "scalar", "sync", "gpsimd")}

    def sem(self, name):
        return Sem(self.es, self.nc, name)

    def op(self, eng, fn, inc=None, k=1):
        if inc is not None:
            inc.n += k
            h = inc.h

            def wrapped(e, fn=fn, h=h, kk=k):
                fn(e).then_inc(h, kk)

            self.streams[eng].append(wrapped)
        else:
            self.streams[eng].append(lambda e, fn=fn: fn(e))

    def w(self, eng, sem, n=None):
        v = sem.n if n is None else n
        if v > 0:
            self.streams[eng].append(lambda e, h=sem.h, v=v: e.wait_ge(h, v))

    def wf(self, eng, free):
        if free is not None:
            self.w(eng, free[0], free[1])

    def emit(self):
        with self.nc.Block(no_gpsimd_drain=True) as blk:
            @blk.tensor
            def _(t):
                for fn in self.streams["tensor"]:
                    fn(t)

            @blk.vector
            def _(v):
                for fn in self.streams["vector"]:
                    fn(v)

            @blk.scalar
            def _(s):
                for fn in self.streams["scalar"]:
                    fn(s)

            @blk.sync
            def _(s):
                for fn in self.streams["sync"]:
                    fn(s)

            @blk.gpsimd
            def _(g):
                for fn in self.streams["gpsimd"]:
                    fn(g)


def build_program(S=2048, dbg=()):
    assert S % 512 == 0
    TC = S // 128
    NGRP = S // 512
    NP = S // 512
    nc = bass.Bass("TRN2", target_bir_lowering=False)

    def din(name, shape):
        return nc.dram_tensor(name, list(shape), F32, kind="ExternalInput").ap()

    x_hbm = din("x", [S, D])
    qa_wT = din("qa_wT", [D, QL])
    kva_wT = din("kva_wT", [D, KVA])
    qbn_g = din("qbn_g", [4, QLC, 128, 512])
    qbp_g = din("qbp_g", [2, QLC, 128, 512])
    kvbk_l = din("kvbk_l", [H, 128, LAT])
    kvbv_l = din("kvbv_l", [H, 128, LAT])
    ow_l = din("ow_l", [DC, 128, D])
    gate_l = din("gate_l", [IC, 128, D])
    up_l = din("up_l", [IC, 128, D])
    down_c = din("down_c", [DC, 128, INTER])
    qaln_cols_in = din("qaln_cols", [128, QLC])
    kvln_cols_in = din("kvln_cols", [128, LATC])
    gates_c = din("gates_c", [128, IC])
    ups_c = din("ups_c", [128, IC])
    downs_c = din("downs_c", [128, DC])
    ident_in = din("ident", [128, 128])
    mask_in = din("mask", [128, 128])
    rotT_in = din("rotT", [128, 128])
    cos_tm_in = din("cos_tm", [S, ROPE])
    sins_tm_in = din("sins_tm", [S, ROPE])
    cos_fm2_in = din("cos_fm2", [128, S])
    sin_fm2_in = din("sin_fm2", [128, S])

    out_hbm = nc.dram_tensor("out", [S, D], F32, kind="ExternalOutput").ap()
    dbg = set(dbg)

    es = ExitStack()
    with es:
        P = Prog(nc, es)

        def sb(name, shape, dt=BF16, side=None):
            kw = {"side": side} if side else {}
            return es.enter_context(nc.sbuf_tensor(name, list(shape), dt, **kw))

        dWs = [P.sem("dW0"), P.sem("dW1")]
        dGs = [P.sem("dG0"), P.sem("dG1")]
        dSh = P.sem("dSh")
        dC0 = P.sem("dC0")
        dC1 = P.sem("dC1")
        dC2 = P.sem("dC2")
        dCF = P.sem("dCF")
        dOut = P.sem("dOut")
        dDump = P.sem("dDump")
        cW = P.sem("cW")
        cS = P.sem("cS")
        tMM = P.sem("tMM")
        tTR = P.sem("tTR")
        vE = P.sem("vE")
        vStat = P.sem("vStat")
        sAct = P.sem("sAct")

        def dump(name, tile, cols, dt, when_sem, when_n):
            if name not in dbg:
                return
            o = nc.dram_tensor("d_" + name, [128, cols], dt, kind="ExternalOutput").ap()
            P.w("gpsimd", when_sem, when_n)
            P.op("gpsimd", lambda e, t=tile, o=o, c=cols:
                 e.dma_start(out=o, in_=t[:, 0:c]), inc=dDump, k=16)

        cI = sb("cI", [128, 128])
        cMask = sb("cMask", [128, 128])
        stage = [sb(f"stage{i}", [128, 512], F32) for i in range(2)]
        stats = sb("stats", [128, 22], F32)

        stage_free = [None, None]
        stage_i = [0]

        def stage_load(src_ap, cols):
            assert cols <= 512
            i = stage_i[0] % 2
            stage_i[0] += 1
            P.wf("sync", stage_free[i])
            P.op("sync", lambda e, s=src_ap, i=i, c=cols:
                 e.dma_start(out=stage[i][:, 0:c], in_=s), inc=dWs[i], k=16)
            return i, dWs[i].n

        def cast_stage(slot, ld, dst_ap, cols, eng="vector", scale=None):
            P.w(eng, dWs[slot], ld)
            if eng == "scalar":
                if scale is None:
                    P.op(eng, lambda e, i=slot, d=dst_ap, c=cols:
                         e.copy(d, stage[i][:, 0:c]), inc=cS)
                else:
                    P.op(eng, lambda e, i=slot, d=dst_ap, c=cols, s=scale:
                         e.activation(d, stage[i][:, 0:c], AF.Copy, scale=s), inc=cS)
                stage_free[slot] = (cS, cS.n)
                return cS.n
            if scale is None:
                P.op(eng, lambda e, i=slot, d=dst_ap, c=cols:
                     e.tensor_copy(d, stage[i][:, 0:c]), inc=cW)
            else:
                P.op(eng, lambda e, i=slot, d=dst_ap, c=cols, s=scale:
                     e.tensor_scalar_mul(d, stage[i][:, 0:c], s), inc=cW)
            stage_free[slot] = (cW, cW.n)
            return cW.n

        def load_cast(src_ap, dst_tile, dst_c0, cols, eng="vector", scale=None):
            done = 0
            c0 = 0
            while c0 < cols:
                cc = min(512, cols - c0)
                sl, ld = stage_load(src_ap[:, c0:c0 + cc], cc)
                done = cast_stage(sl, ld, dst_tile[:, dst_c0 + c0:dst_c0 + c0 + cc],
                                  cc, eng, scale)
                c0 += cc
            return done

        def mk_path(sb_fn, name, eng, cast_eng, nslots=2, slot_cols=1024):
            tiles = [sb_fn(f"stg_{name}{i}", [128, slot_cols], F32) for i in range(nslots)]
            sems = [P.sem(f"d_{name}{i}") for i in range(nslots)]
            cP = P.sem(f"c_{name}")
            st = {"free": [None] * nslots, "i": 0}

            def lc(src_ap, dst_tile, dst_c0, cols, scale=None):
                done = None
                c0 = 0
                while c0 < cols:
                    cc = min(slot_cols, cols - c0)
                    i = st["i"] % nslots
                    st["i"] += 1
                    P.wf(eng, st["free"][i])
                    P.op(eng, lambda e, s=src_ap[:, c0:c0 + cc], i=i, cc=cc:
                         e.dma_start(out=tiles[i][:, 0:cc], in_=s), inc=sems[i], k=16)
                    ld = sems[i].n
                    P.w(cast_eng, sems[i], ld)
                    if cast_eng == "scalar":
                        if scale is None:
                            P.op(cast_eng, lambda e, i=i, d=dst_tile, a=dst_c0 + c0, cc=cc:
                                 e.copy(d[:, a:a + cc], tiles[i][:, 0:cc]), inc=cP)
                        else:
                            P.op(cast_eng, lambda e, i=i, d=dst_tile, a=dst_c0 + c0,
                                 cc=cc, sc=scale:
                                 e.activation(d[:, a:a + cc], tiles[i][:, 0:cc],
                                              AF.Copy, scale=sc), inc=cP)
                    else:
                        if scale is None:
                            P.op(cast_eng, lambda e, i=i, d=dst_tile, a=dst_c0 + c0, cc=cc:
                                 e.tensor_copy(d[:, a:a + cc], tiles[i][:, 0:cc]), inc=cP)
                        else:
                            P.op(cast_eng, lambda e, i=i, d=dst_tile, a=dst_c0 + c0,
                                 cc=cc, sc=scale:
                                 e.tensor_scalar_mul(d[:, a:a + cc],
                                                     tiles[i][:, 0:cc], sc), inc=cP)
                    st["free"][i] = (cP, cP.n)
                    done = (cP, cP.n)
                    c0 += cc
                return done
            return lc

        def gate():
            for eng in ("tensor", "vector", "scalar", "sync", "gpsimd"):
                for s in (tMM, tTR, vE, vStat, sAct, cW, cS, dDump, dSh):
                    P.w(eng, s, s.n)

        load_cast(ident_in, cI, 0, 128)
        load_cast(mask_in, cMask, 0, 128)
        consts_done = cW.n

        qln_c = sb("qln_c", [128, QLC], F32)
        kln_c = sb("kln_c", [128, LATC], F32)
        P.op("sync", lambda e: e.dma_start(out=qln_c[:], in_=qaln_cols_in), inc=dC0, k=16)
        P.op("sync", lambda e: e.dma_start(out=kln_c[:], in_=kvln_cols_in), inc=dC0, k=16)
        ln_cols_ld = dC0.n

        scl = sb("scl", [128, 1], F32)
        rql = sb("rql", [128, 1], F32)
        P.op("vector", lambda e: e.memset(scl[:], float(QHD)), inc=vStat)
        P.op("vector", lambda e: e.memset(rql[:], float(QL)), inc=vStat)
        P.w("vector", vStat, vStat.n)
        P.op("vector", lambda e: e.reciprocal(scl[:], scl[:]), inc=vStat)
        P.op("vector", lambda e: e.reciprocal(rql[:], rql[:]), inc=vStat)
        P.w("scalar", vStat, vStat.n)
        P.op("scalar", lambda e: e.activation(scl[:], scl[:], AF.Sqrt), inc=sAct)
        scl_done = sAct.n

        rqStack = ExitStack()
        oxStack = ExitStack()
        OX = None

        a1_done_v = q_done_v = heads_done_v = kpe_hi_done = 0
        with ExitStack() as phA:
            def sbA(name, shape, dt=BF16):
                return phA.enter_context(nc.sbuf_tensor(name, list(shape), dt))

            c_fm = sbA("c_fm", [128, LATC * S])
            kpe_lo = sbA("kpe_lo", [128, S])
            rmsqa_fm = rqStack.enter_context(
                nc.sbuf_tensor("rmsqa_fm", [128, QLC * S], BF16, side="right"))

            P.op("vector", lambda e: e.memset(kpe_lo[64:128, :], 0.0), inc=vE)
            kpe_zeroed = vE.n

            with ExitStack() as ph1:
                def sb1(name, shape, dt=BF16):
                    return ph1.enter_context(nc.sbuf_tensor(name, list(shape), dt))

                def ps1(name, shape):
                    return ph1.enter_context(nc.psum_tensor(name, list(shape), F32))

                qa_w_bf = sb1("qa_w_bf", [128, DC * QL])
                kva_w_bf = sb1("kva_w_bf", [128, DC * KVA])
                cos_tm = sb1("cos_tm_sb", [128, TC * ROPE], F32)
                sins_tm = sb1("sins_tm_sb", [128, TC * ROPE], F32)
                x_tm_bf = [sb1(f"x_tm_bf{i}", [128, D]) for i in range(2)]
                x0fm_t = [sb1(f"x0fm_t{i}", [128, D]) for i in range(2)]
                rq_tm = [sb1(f"rq_tm{i}", [128, QL]) for i in range(2)]
                ckv_tm = [sb1(f"ckv_tm{i}", [128, LAT]) for i in range(2)]
                kpe_sc = sb1("kpe_sc", [128, ROPE], F32)
                kpe_r1 = sb1("kpe_r1", [128, ROPE], F32)
                kpe_r2 = sb1("kpe_r2", [128, ROPE], F32)
                kpe_bf = sb1("kpe_bf", [128, ROPE])
                junk = sb1("junk", [128, 2048])

                pQA = ps1("pQA", [128, QL])
                pCKV = ps1("pCKV", [128, LAT])
                pKPE = ps1("pKPE", [128, 64])
                pTR = ps1("pTR", [128, 512])
                pTR2 = ps1("pTR2", [128, 512])

                for t in range(TC):
                    P.op("sync", lambda e, t=t: e.dma_start(
                        out=cos_tm[:, t * ROPE:(t + 1) * ROPE],
                        in_=cos_tm_in[t * 128:(t + 1) * 128, :]), inc=dC1, k=16)
                    P.op("sync", lambda e, t=t: e.dma_start(
                        out=sins_tm[:, t * ROPE:(t + 1) * ROPE],
                        in_=sins_tm_in[t * 128:(t + 1) * 128, :]), inc=dC1, k=16)
                tm_ld = dC1.n

                for d in range(DC):
                    load_cast(qa_wT[d * 128:(d + 1) * 128, :], qa_w_bf, d * QL, QL)
                for d in range(DC):
                    load_cast(kva_wT[d * 128:(d + 1) * 128, :], kva_w_bf, d * KVA, KVA)
                a1_w = cW.n

                x_free = [None, None]
                epi_free = [None, None]
                x0_free = [None, None]
                pqa_free = None
                pckv_free = None
                pkpe_free = None
                ptr_free = None

                x_cast_t = [None] * (TC + 1)
                P.wf("vector", epi_free[0])
                x_cast_t[0] = load_cast(x_hbm[0:128, :], x_tm_bf[0], 0, D)
                ptrb = [pTR, pTR2]
                ptr_free2 = [None, None]
                gctr = 0
                pend_epi = None

                def emit_a1_epi(pi, pt, pmm):
                    nonlocal pqa_free, pckv_free, pkpe_free
                    P.w("scalar", sAct, sAct.n)
                    P.w("scalar", tMM, pmm)
                    P.op("scalar", lambda e: e.activation(
                        junk[:, 0:QL], pQA[:], AF.Square,
                        accum_out=stats[:, 0:1]), inc=sAct)
                    sq_qa = sAct.n
                    P.op("scalar", lambda e: e.activation(
                        junk[:, 1536:1536 + LAT], pCKV[:], AF.Square,
                        accum_out=stats[:, 3:4]), inc=sAct)
                    sq_ckv = sAct.n
                    P.w("vector", sAct, sq_qa)
                    P.op("vector", lambda e: e.tensor_scalar(
                        stats[:, 1:2], stats[:, 0:1], rql[:], EPS_RMS,
                        op0=ALU.mult, op1=ALU.add), inc=vStat)
                    P.w("vector", sAct, sq_ckv)
                    P.op("vector", lambda e: e.tensor_scalar(
                        stats[:, 4:5], stats[:, 3:4], 1.0 / LAT, EPS_RMS,
                        op0=ALU.mult, op1=ALU.add), inc=vStat)
                    P.w("vector", vStat, vStat.n)
                    P.op("vector", lambda e: e.reciprocal(stats[:, 1:2], stats[:, 1:2]), inc=vStat)
                    P.op("vector", lambda e: e.reciprocal(stats[:, 4:5], stats[:, 4:5]), inc=vStat)
                    P.w("scalar", vStat, vStat.n)
                    P.op("scalar", lambda e: e.activation(
                        stats[:, 2:3], stats[:, 1:2], AF.Sqrt), inc=sAct)
                    P.op("scalar", lambda e: e.activation(
                        stats[:, 5:6], stats[:, 4:5], AF.Sqrt), inc=sAct)
                    P.w("scalar", sAct, sAct.n)
                    P.op("scalar", lambda e, pi=pi: e.activation(
                        rq_tm[pi][:], pQA[:], AF.Copy,
                        scale=stats[:, 2:3]), inc=sAct)
                    pqa_free = (sAct, sAct.n)
                    P.op("scalar", lambda e, pi=pi: e.activation(
                        ckv_tm[pi][:], pCKV[:], AF.Copy,
                        scale=stats[:, 5:6]), inc=sAct)
                    pckv_free = (sAct, sAct.n)
                    P.w("vector", dC1, tm_ld)
                    P.w("vector", tMM, pmm)
                    P.op("vector", lambda e: e.tensor_copy(kpe_sc[:], pKPE[:]), inc=vE)
                    pkpe_free = (vE, vE.n)
                    P.w("vector", vE, vE.n)
                    P.op("vector", lambda e, pt=pt: e.tensor_mul(
                        kpe_r1[:], kpe_sc[:], cos_tm[:, pt * ROPE:(pt + 1) * ROPE]), inc=vE)
                    swap_ap = bass.AP(kpe_sc, 32, [[ROPE, 128], [-32, 2], [1, 32]])
                    P.op("vector", lambda e, pt=pt, sw=swap_ap: e.tensor_tensor(
                        kpe_r2[:].rearrange("p (a b) -> p a b", a=2), sw,
                        sins_tm[:, pt * ROPE:(pt + 1) * ROPE].rearrange("p (a b) -> p a b", a=2),
                        ALU.mult), inc=vE)
                    P.w("vector", vE, vE.n)
                    P.op("vector", lambda e: e.tensor_add(
                        kpe_bf[:], kpe_r1[:], kpe_r2[:]), inc=vE)
                    return vE.n

                def emit_a1_tr(pi, pt, pkpe_done):
                    nonlocal gctr
                    P.w("tensor", vE, pkpe_done)
                    P.wf("tensor", pqa_free)
                    P.wf("tensor", pckv_free)
                    for k4 in range(3):
                        b = gctr % 2
                        gctr += 1
                        P.wf("tensor", ptr_free2[b])
                        for kk in range(4):
                            k = k4 * 4 + kk
                            P.op("tensor", lambda e, pi=pi, k=k, kk=kk, b=b: e.matmul(
                                ptrb[b][:, kk * 128:(kk + 1) * 128],
                                rq_tm[pi][:, k * 128:(k + 1) * 128], cI[:],
                                start=True, stop=True), inc=(tTR if kk == 3 else None))
                        P.w("vector", tTR, tTR.n)
                        oa = bass.AP(rmsqa_fm, (k4 * 4) * S + pt * 128,
                                     [[QLC * S, 128], [S, 4], [1, 128]])
                        P.op("vector", lambda e, oa=oa, b=b: e.tensor_copy(
                            oa, ptrb[b][:].rearrange("p (a b) -> p a b", a=4)), inc=vE)
                        ptr_free2[b] = (vE, vE.n)
                    b = gctr % 2
                    gctr += 1
                    P.wf("tensor", ptr_free2[b])
                    for kk in range(4):
                        P.op("tensor", lambda e, pi=pi, kk=kk, b=b: e.matmul(
                            ptrb[b][:, kk * 128:(kk + 1) * 128],
                            ckv_tm[pi][:, kk * 128:(kk + 1) * 128], cI[:],
                            start=True, stop=True), inc=(tTR if kk == 3 else None))
                    P.w("vector", tTR, tTR.n)
                    oa = bass.AP(c_fm, pt * 128, [[LATC * S, 128], [S, 4], [1, 128]])
                    P.op("vector", lambda e, oa=oa, b=b: e.tensor_copy(
                        oa, ptrb[b][:].rearrange("p (a b) -> p a b", a=4)), inc=vE)
                    ptr_free2[b] = (vE, vE.n)
                    b = gctr % 2
                    gctr += 1
                    P.wf("tensor", ptr_free2[b])
                    P.op("tensor", lambda e, b=b: e.matmul(
                        ptrb[b][0:64, 0:128], kpe_bf[:], cI[:],
                        start=True, stop=True), inc=tTR)
                    P.w("vector", tTR, tTR.n)
                    P.w("vector", vE, kpe_zeroed)
                    P.op("vector", lambda e, pt=pt, b=b: e.tensor_copy(
                        kpe_lo[0:64, pt * 128:(pt + 1) * 128], ptrb[b][0:64, 0:128]), inc=vE)
                    ptr_free2[b] = (vE, vE.n)

                for t in range(TC):
                    i = t % 2
                    x_cast = x_cast_t[t]
                    P.w("tensor", cW, max(x_cast, consts_done))
                    for d4 in range(4):
                        b = gctr % 2
                        gctr += 1
                        P.wf("tensor", ptr_free2[b])
                        for dd in range(4):
                            d = d4 * 4 + dd
                            P.op("tensor", lambda e, i=i, d=d, dd=dd, b=b: e.matmul(
                                ptrb[b][:, dd * 128:(dd + 1) * 128],
                                x_tm_bf[i][:, d * 128:(d + 1) * 128], cI[:],
                                start=True, stop=True), inc=(tTR if dd == 3 else None))
                        P.w("vector", tTR, tTR.n)
                        P.wf("vector", x0_free[i])
                        P.op("vector", lambda e, i=i, d4=d4, b=b: e.tensor_copy(
                            x0fm_t[i][:, d4 * 512:(d4 + 1) * 512], ptrb[b][:]), inc=vE)
                        ptr_free2[b] = (vE, vE.n)
                    x_fm_done = vE.n
                    epi_free[i] = (tTR, tTR.n)
                    if t + 1 < TC:
                        ni = (t + 1) % 2
                        P.wf("vector", epi_free[ni])
                        x_cast_t[t + 1] = load_cast(
                            x_hbm[(t + 1) * 128:(t + 2) * 128, :], x_tm_bf[ni], 0, D)
                    if pend_epi is not None:
                        kd = emit_a1_epi(*pend_epi)
                        emit_a1_tr(pend_epi[0], pend_epi[1], kd)
                    P.w("tensor", cW, a1_w)
                    P.w("tensor", vE, x_fm_done)
                    P.wf("tensor", pqa_free)
                    P.wf("tensor", pckv_free)
                    P.wf("tensor", pkpe_free)
                    for d in range(DC):
                        st, sp = (d == 0), (d == DC - 1)
                        for j in range(3):
                            P.op("tensor", lambda e, i=i, d=d, j=j, st=st, sp=sp: e.matmul(
                                pQA[:, j * 512:(j + 1) * 512],
                                x0fm_t[i][:, d * 128:(d + 1) * 128],
                                qa_w_bf[:, d * QL + j * 512:d * QL + (j + 1) * 512],
                                start=st, stop=sp))
                        P.op("tensor", lambda e, i=i, d=d, st=st, sp=sp: e.matmul(
                            pCKV[:], x0fm_t[i][:, d * 128:(d + 1) * 128],
                            kva_w_bf[:, d * KVA:d * KVA + LAT],
                            start=st, stop=sp))
                        P.op("tensor", lambda e, i=i, d=d, st=st, sp=sp: e.matmul(
                            pKPE[:], x0fm_t[i][:, d * 128:(d + 1) * 128],
                            kva_w_bf[:, d * KVA + LAT:d * KVA + KVA],
                            start=st, stop=sp), inc=(tMM if sp else None))
                    a1_mm = tMM.n
                    x0_free[i] = (tMM, a1_mm)
                    pend_epi = (i, t, a1_mm)

                kd = emit_a1_epi(*pend_epi)
                emit_a1_tr(pend_epi[0], pend_epi[1], kd)
                a1_done_v = vE.n
                kpe_hi_done = 0

            dump("rmsqa", rmsqa_fm, QLC * S, BF16, vE, a1_done_v)
            dump("cfm", c_fm, LATC * S, BF16, vE, a1_done_v)
            dump("kpelo", kpe_lo, S, BF16, vE, a1_done_v)

            Q_all = sbA("Q_all", [128, 24 * S])
            gate()
            with ExitStack() as ph2:
                def sb2(name, shape, dt=BF16):
                    return ph2.enter_context(nc.sbuf_tensor(name, list(shape), dt))

                def ps2(name, shape):
                    return ph2.enter_context(nc.psum_tensor(name, list(shape), F32))

                qbw_bf = sb2("qbw_bf", [128, 18 * 512])
                cRotT = sb2("cRotT", [128, 128])
                cosf = sb2("cosf", [128, S])
                sinf = sb2("sinf", [128, S])
                pe_raw = sb2("pe_raw", [128, 512])
                t1 = sb2("t1", [128, 512], F32)
                t2 = sb2("t2", [128, 512], F32)
                t3 = sb2("t3", [128, 512], F32)

                pW = [ps2("pW0", [128, 512]), ps2("pW1", [128, 512])]
                pROT = ps2("pROT", [128, 512])
                lc_a2 = mk_path(sb2, "a2", "scalar", "scalar",
                                nslots=2, slot_cols=512)

                load_cast(cos_fm2_in, cosf, 0, S)
                load_cast(sin_fm2_in, sinf, 0, S)
                load_cast(rotT_in, cRotT, 0, 128)
                fm2_ld = cW.n

                pw_free = [None, None]
                prot_free = None
                qbw_free = None
                pe_raw_free = [None]
                t1_free = [None]
                t2_free = [None]

                slot_free = [None] * 18
                gidx = [0]

                def q_group(src, grp, chunks, is_pe):
                    nonlocal prot_free, qbw_free
                    g = gidx[0]
                    gidx[0] += 1
                    slots = [(g * QLC + k) % 18 for k in range(QLC)]
                    w_cp = None
                    for k in range(QLC):
                        if k % 2 == 0:
                            P.w("vector", dC0, ln_cols_ld)
                            P.wf("vector", slot_free[slots[k]])
                            load_cast(src[grp, k], qbw_bf, slots[k] * 512, 512,
                                      scale=qln_c[:, k:k + 1])
                        else:
                            P.w("scalar", dC0, ln_cols_ld)
                            P.wf("scalar", slot_free[slots[k]])
                            w_cp = lc_a2(src[grp, k], qbw_bf, slots[k] * 512, 512,
                                         scale=qln_c[:, k:k + 1])
                    w_ready = cW.n
                    for cc, ch in enumerate(chunks):
                        for piece in range(NP):
                            pb = (cc * NP + piece) % 2
                            P.w("tensor", cW, w_ready)
                            P.wf("tensor", w_cp)
                            P.w("tensor", vE, a1_done_v)
                            P.wf("tensor", pw_free[pb])
                            for k in range(QLC):
                                P.op("tensor", lambda e, pb=pb, sk=slots[k], k=k, cc=cc, piece=piece: e.matmul(
                                    pW[pb][:],
                                    qbw_bf[:, sk * 512 + cc * 128:sk * 512 + (cc + 1) * 128],
                                    rmsqa_fm[:, k * S + piece * 512:k * S + (piece + 1) * 512],
                                    start=(k == 0), stop=(k == QLC - 1)),
                                    inc=(tMM if k == QLC - 1 else None))
                            mm = tMM.n
                            P.w("vector", tMM, mm)
                            if not is_pe:
                                P.w("vector", sAct, scl_done)
                                P.op("vector", lambda e, pb=pb, ch=ch, piece=piece: e.tensor_scalar_mul(
                                    Q_all[:, ch * S + piece * 512:ch * S + (piece + 1) * 512],
                                    pW[pb][:], scl[:]), inc=vE)
                                pw_free[pb] = (vE, vE.n)
                            else:
                                P.wf("vector", pe_raw_free[0])
                                P.op("vector", lambda e, pb=pb: e.tensor_copy(
                                    pe_raw[:], pW[pb][:]), inc=vE)
                                P.w("vector", cW, fm2_ld)
                                P.wf("vector", t1_free[0])
                                P.op("vector", lambda e, pb=pb, piece=piece: e.tensor_mul(
                                    t1[:], pW[pb][:],
                                    cosf[:, piece * 512:(piece + 1) * 512]), inc=vE)
                                pw_free[pb] = (vE, vE.n)
                                pe_ev = vE.n
                                P.w("tensor", vE, pe_ev)
                                P.w("tensor", cW, fm2_ld)
                                P.wf("tensor", prot_free)
                                P.op("tensor", lambda e: e.matmul(
                                    pROT[:], cRotT[:], pe_raw[:],
                                    start=True, stop=True), inc=tMM)
                                pe_raw_free[0] = (tMM, tMM.n)
                                P.w("vector", tMM, tMM.n)
                                P.wf("vector", t2_free[0])
                                P.op("vector", lambda e, piece=piece: e.tensor_mul(
                                    t2[:], pROT[:],
                                    sinf[:, piece * 512:(piece + 1) * 512]), inc=vE)
                                prot_free = (vE, vE.n)
                                P.w("vector", vE, vE.n)
                                P.w("vector", sAct, scl_done)
                                P.op("vector", lambda e: e.tensor_add(t3[:], t1[:], t2[:]), inc=vE)
                                t1_free[0] = (vE, vE.n)
                                t2_free[0] = (vE, vE.n)
                                P.w("vector", vE, vE.n)
                                P.op("vector", lambda e, ch=ch, piece=piece: e.tensor_scalar_mul(
                                    Q_all[:, ch * S + piece * 512:ch * S + (piece + 1) * 512],
                                    t3[:], scl[:]), inc=vE)
                    for sk in slots:
                        slot_free[sk] = (tMM, tMM.n)
                    qbw_free = (tMM, tMM.n)

                for grp in range(4):
                    q_group(qbn_g, grp, [4 * grp + j for j in range(4)], False)
                for grp in range(2):
                    q_group(qbp_g, grp, [16 + 4 * grp + j for j in range(4)], True)
                q_done_v = vE.n
            rqStack.close()

            dump("qall", Q_all, 24 * S, BF16, vE, q_done_v)

            OX = oxStack.enter_context(
                nc.sbuf_tensor("OX", [128, DC * S], BF16, side="right"))
            gate()
            with ExitStack() as ph3:
                def sb3(name, shape, dt=BF16):
                    return ph3.enter_context(nc.sbuf_tensor(name, list(shape), dt))

                def ps3(name, shape):
                    return ph3.enter_context(nc.psum_tensor(name, list(shape), F32))

                kfm = sb3("kfm", [128, S])
                vtm = sb3("vtm", [128, S])
                kvw3 = [sb3(f"kvw3_{i}", [128, LAT]) for i in range(3)]
                Ebuf = [sb3(f"Ebuf{i}", [128, S]) for i in range(2)]
                PTsb = [sb3(f"PTsb{i}", [128, 512]) for i in range(2)]
                Dg = [sb3(f"Dg{i}", [128, 128]) for i in range(2)]

                pPT = [ps3("pPT0", [128, 512]), ps3("pPT1", [128, 512])]
                pS = ps3("pS", [128, S])
                pX = ps3("pX", [128, 512])
                pO = ps3("pO", [128, 512])
                ring = [(pS, r * 512) for r in range(S // 512)] + [(pX, 0)]
                NR = len(ring)
                pW = pPT

                kv_free = None
                ppt_free = [None, None]
                pw_free = ppt_free
                e_free = [None, None]
                dg_free = [None, None]
                region_free = [None] * (S // 512 + 1)
                rp = [0]
                po_free = None
                pts_free = [None, None]

                def kv_load(src, dst):
                    sl, ld = stage_load(src, LAT)
                    P.w("vector", dWs[sl], ld)
                    P.w("vector", dC0, ln_cols_ld)
                    for l in range(LATC):
                        P.op("vector", lambda e, sl=sl, dst=dst, l=l: e.tensor_scalar_mul(
                            dst[:, l * 128:(l + 1) * 128],
                            stage[sl][:, l * 128:(l + 1) * 128],
                            kln_c[:, l:l + 1]), inc=cW)
                    stage_free[sl] = (cW, cW.n)
                    return cW.n

                kvw_ready = {}
                kvw_ready[(0, 0)] = kv_load(kvbk_l[0], kvw3[0])
                kvw_ready[(0, 1)] = kv_load(kvbv_l[0], kvw3[1])
                vmm_prev = [0]

                def emit_kprod(h):
                    kw_sb = kvw3[(2 * h) % 3]
                    P.w("tensor", cW, kvw_ready[(h, 0)])
                    P.w("tensor", vE, a1_done_v)
                    kpr = [None] * NP
                    for piece in range(NP):
                        pb = piece % 2
                        P.wf("tensor", pw_free[pb])
                        for l in range(LATC):
                            P.op("tensor", lambda e, pb=pb, l=l, piece=piece, w=kw_sb: e.matmul(
                                pW[pb][:],
                                w[:, l * 128:(l + 1) * 128],
                                c_fm[:, l * S + piece * 512:l * S + (piece + 1) * 512],
                                start=(l == 0), stop=(l == LATC - 1)),
                                inc=(tMM if l == LATC - 1 else None))
                        P.w("vector", tMM, tMM.n)
                        P.op("vector", lambda e, pb=pb, piece=piece: e.tensor_copy(
                            kfm[:, piece * 512:(piece + 1) * 512], pW[pb][:]), inc=vE)
                        pw_free[pb] = (vE, vE.n)
                        kpr[piece] = (vE, vE.n)
                    kmm = tMM.n
                    if h + 1 < H:
                        P.w("vector", tMM, vmm_prev[0])
                        kvw_ready[(h + 1, 0)] = kv_load(
                            kvbk_l[h + 1], kvw3[(2 * h + 2) % 3])
                    return kpr, kmm

                def emit_vprod(h, kmm):
                    vw_sb = kvw3[(2 * h + 1) % 3]
                    P.w("tensor", cW, kvw_ready[(h, 1)])
                    for tch in range(TC):
                        pb = tch % 2
                        P.wf("tensor", pw_free[pb])
                        for l in range(LATC):
                            P.op("tensor", lambda e, pb=pb, l=l, tch=tch, w=vw_sb: e.matmul(
                                pW[pb][:, 0:128],
                                c_fm[:, l * S + tch * 128:l * S + (tch + 1) * 128],
                                w[:, l * 128:(l + 1) * 128],
                                start=(l == 0), stop=(l == LATC - 1)),
                                inc=(tMM if l == LATC - 1 else None))
                        P.w("vector", tMM, tMM.n)
                        P.op("vector", lambda e, pb=pb, tch=tch: e.tensor_copy(
                            vtm[:, tch * 128:(tch + 1) * 128], pW[pb][:, 0:128]), inc=vE)
                        pw_free[pb] = (vE, vE.n)
                    vmm_prev[0] = tMM.n
                    if h + 1 < H:
                        P.w("vector", tMM, kmm)
                        kvw_ready[(h + 1, 1)] = kv_load(
                            kvbv_l[h + 1], kvw3[(2 * h) % 3])
                    return (vE, vE.n)

                kpr_n, kmm_n = emit_kprod(0)
                kvr_n = emit_vprod(0, kmm_n)

                kpe_shift_done = 0
                for h in range(H):
                    kpr, kv_ready = kpr_n, kvr_n
                    if h == 8:
                        P.w("gpsimd", tMM, tMM.n)
                        P.op("gpsimd", lambda e: e.dma_start(
                            out=kpe_lo[64:128, :], in_=kpe_lo[0:64, :]), inc=dSh, k=16)
                        P.w("vector", dSh, dSh.n)
                        P.op("vector", lambda e: e.memset(kpe_lo[0:64, :], 0.0), inc=vE)
                        kpe_shift_done = vE.n

                    kpe_t = kpe_lo
                    pch = 16 + h % 8

                    pendq = []
                    for qc in range(TC + 2):
                        if qc < TC:
                            ei = qc % 2
                            kcols = (qc + 1) * 128
                            ngr = (kcols + 511) // 512
                            par = 11 * (qc % 2)
                            regs = [(rp[0] + g) % NR for g in range(ngr)]
                            rp[0] = (rp[0] + ngr) % NR
                            P.w("tensor", vE, q_done_v)
                            P.w("tensor", vE, max(kpe_shift_done, kpe_zeroed))
                            gmm = []
                            for g in range(ngr):
                                n = min(512, kcols - g * 512)
                                rt, off = ring[regs[g]]
                                P.wf("tensor", kpr[g])
                                P.wf("tensor", region_free[regs[g]])
                                P.op("tensor", lambda e, h=h, qc=qc, g=g, n=n, rt=rt, off=off: e.matmul(
                                    rt[:, off:off + n],
                                    Q_all[:, h * S + qc * 128:h * S + (qc + 1) * 128],
                                    kfm[:, g * 512:g * 512 + n],
                                    start=True, stop=False))
                                if g == ngr - 1:
                                    moff = off + (qc % 4) * 128
                                    P.op("tensor", lambda e, rt=rt, moff=moff: e.matmul(
                                        rt[:, moff:moff + 128],
                                        cI[:], cMask[:], start=False, stop=False))
                                P.op("tensor", lambda e, qc=qc, g=g, n=n, pch=pch,
                                     kp=kpe_t, rt=rt, off=off: e.matmul(
                                    rt[:, off:off + n],
                                    Q_all[:, pch * S + qc * 128:pch * S + (qc + 1) * 128],
                                    kp[:, g * 512:g * 512 + n],
                                    start=False, stop=True), inc=tMM)
                                gmm.append(tMM.n)
                                P.w("vector", tMM, gmm[g])
                                P.op("vector", lambda e, g=g, n=n, rt=rt, off=off, par=par: e.reduce_max(
                                    stats[:, par + g:par + g + 1], rt[:, off:off + n],
                                    axis=AX), inc=vStat)
                        if len(pendq) >= 2 or (qc >= TC and pendq):
                            pqc, pdg, pexp = pendq.pop(0)
                            pei = pqc % 2
                            nkt = pqc + 1
                            first_pv = True
                            pvq = None

                            def emit_pv(sl, b0, bn, ev):
                                nonlocal first_pv
                                P.w("tensor", sAct, ev)
                                if first_pv:
                                    P.wf("tensor", kv_ready)
                                    P.wf("tensor", po_free)
                                    first_pv = False
                                for bb in range(bn):
                                    kt = b0 + bb
                                    P.op("tensor", lambda e, kt=kt, bb=bb, sl=sl, nkt=nkt: e.matmul(
                                        pO[:, 0:128],
                                        vtm[:, kt * 128:(kt + 1) * 128],
                                        PTsb[sl][:, bb * 128:(bb + 1) * 128],
                                        start=(kt == 0), stop=(kt == nkt - 1)),
                                        inc=(tMM if kt == nkt - 1 else None))
                                pts_free[sl] = (tMM, tMM.n)

                            for b0 in range(0, nkt, 4):
                                bn = min(4, nkt - b0)
                                sl = (b0 // 4) % 2
                                P.w("tensor", vStat, pdg)
                                P.w("tensor", sAct, pexp)
                                P.wf("tensor", ppt_free[sl])
                                for bb in range(bn):
                                    kt = b0 + bb
                                    P.op("tensor", lambda e, pei=pei, kt=kt, bb=bb, sl=sl: e.matmul(
                                        pPT[sl][:, bb * 128:(bb + 1) * 128],
                                        Ebuf[pei][:, kt * 128:(kt + 1) * 128], Dg[pei][:],
                                        start=True, stop=True),
                                        inc=(tTR if bb == bn - 1 else None))
                                P.w("scalar", tTR, tTR.n)
                                P.wf("scalar", pts_free[sl])
                                P.op("scalar", lambda e, bn=bn, sl=sl: e.copy(
                                    PTsb[sl][:, 0:bn * 128], pPT[sl][:, 0:bn * 128]), inc=sAct)
                                ppt_free[sl] = (sAct, sAct.n)
                                if pvq is not None:
                                    emit_pv(*pvq)
                                pvq = (sl, b0, bn, sAct.n)
                            emit_pv(*pvq)
                            o_mm = tMM.n
                            e_free[pei] = (tTR, tTR.n)
                            dg_free[pei] = (tTR, tTR.n)
                            P.w("scalar", tMM, o_mm)
                            P.op("scalar", lambda e, h=h, pqc=pqc: e.copy(
                                OX[:, h * S + pqc * 128:h * S + (pqc + 1) * 128],
                                pO[:, 0:128]), inc=sAct)
                            po_free = (sAct, sAct.n)
                            if pqc == TC - 2 and h + 1 < H:
                                kpr_n, kmm_n = emit_kprod(h + 1)
                        if qc < TC:
                            P.w("vector", vStat, vStat.n)
                            P.op("vector", lambda e, ngr=ngr, par=par: e.reduce_max(
                                stats[:, par + 4:par + 5], stats[:, par:par + ngr],
                                axis=AX, negate=True), inc=vStat)
                            neg_done = vStat.n
                            P.wf("scalar", e_free[ei])
                            P.w("scalar", vStat, neg_done)
                            for g in range(ngr):
                                n = min(512, kcols - g * 512)
                                rt, off = ring[regs[g]]
                                P.op("scalar", lambda e, ei=ei, g=g, n=n, rt=rt, off=off, par=par: e.activation(
                                    Ebuf[ei][:, g * 512:g * 512 + n], rt[:, off:off + n],
                                    AF.Exp, bias=stats[:, par + 4:par + 5],
                                    accum_out=stats[:, par + 5 + g:par + 6 + g]), inc=sAct)
                                region_free[regs[g]] = (sAct, sAct.n)
                            exp_done = sAct.n
                            P.w("vector", sAct, exp_done)
                            P.op("vector", lambda e, ngr=ngr, par=par: e.reduce_sum(
                                stats[:, par + 9:par + 10], stats[:, par + 5:par + 5 + ngr],
                                axis=AX), inc=vStat)
                            P.w("vector", vStat, vStat.n)
                            P.op("vector", lambda e, par=par: e.reciprocal(
                                stats[:, par + 10:par + 11], stats[:, par + 9:par + 10]), inc=vStat)
                            P.w("vector", vStat, vStat.n)
                            P.wf("vector", dg_free[ei])
                            P.op("vector", lambda e, ei=ei, par=par: e.tensor_scalar_mul(
                                Dg[ei][:], cI[:], stats[:, par + 10:par + 11]), inc=vStat)
                            pendq.append((qc, vStat.n, exp_done))
                    if h + 1 < H:
                        kvr_n = emit_vprod(h + 1, kmm_n)

                heads_done_s = sAct.n

            dump("ofm", OX, DC * S, BF16, sAct, heads_done_s)

        a4_done_v = 0
        gate()
        with ExitStack() as ph4:
            def sb4(name, shape, dt=BF16):
                return ph4.enter_context(nc.sbuf_tensor(name, list(shape), dt))

            def ps4(name, shape):
                return ph4.enter_context(nc.psum_tensor(name, list(shape), F32))

            ow_bf = sb4("ow_bf", [128, DC * D])
            x2 = sb4("x2", [128, D], F32)
            x1tm = [sb4(f"x1tm{i}", [128, D]) for i in range(2)]
            xst4 = [sb4(f"xst4_{i}", [128, D], F32) for i in range(2)]
            junk4 = sb4("junk4", [128, D])

            pOP = ps4("pOP", [128, D])
            pTR4 = ps4("pTR4", [128, 512])
            lc_a4 = mk_path(sb4, "a4", "scalar", "scalar",
                            nslots=2, slot_cols=1024)

            x4_free = [None, None]
            x_ld4 = [None] * TC
            for t in range(2):
                P.op("gpsimd", lambda e, t=t, i=t % 2: e.dma_start(
                    out=xst4[i][:], in_=x_hbm[t * 128:(t + 1) * 128, :]),
                    inc=dGs[t % 2], k=16)
                x_ld4[t] = dGs[t % 2].n

            ow_cast = []
            for d in range(DC):
                if d % 2 == 0:
                    ow_cast.append((cW, load_cast(ow_l[d], ow_bf, d * D, D)))
                else:
                    ow_cast.append(lc_a4(ow_l[d], ow_bf, d * D, D))

            pop_free = None
            ptr4_free = None
            x1tm_free = [None, None]
            x2_free = None
            pend_tr = None

            def emit_x1_transposes(pt, pi, papply):
                nonlocal ptr4_free
                P.w("tensor", vE, papply)
                for d4 in range(4):
                    P.wf("tensor", ptr4_free)
                    for dd in range(4):
                        d = d4 * 4 + dd
                        P.op("tensor", lambda e, d=d, dd=dd, pi=pi: e.matmul(
                            pTR4[:, dd * 128:(dd + 1) * 128],
                            x1tm[pi][:, d * 128:(d + 1) * 128], cI[:],
                            start=True, stop=True), inc=(tTR if dd == 3 else None))
                    P.w("vector", tTR, tTR.n)
                    oa = bass.AP(OX, (d4 * 4) * S + pt * 128,
                                 [[DC * S, 128], [S, 4], [1, 128]])
                    P.op("vector", lambda e, oa=oa: e.tensor_copy(
                        oa, pTR4[:].rearrange("p (a b) -> p a b", a=4)), inc=vE)
                    ptr4_free = (vE, vE.n)
                x1tm_free[pi] = (tTR, tTR.n)

            for t in range(TC):
                i = t % 2
                x_ld = x_ld4[t]
                P.w("tensor", sAct, heads_done_s)
                P.wf("tensor", pop_free)
                for hv in range(DC):
                    if t == 0:
                        P.wf("tensor", ow_cast[hv])
                    st, sp = (hv == 0), (hv == DC - 1)
                    for j in range(4):
                        P.op("tensor", lambda e, hv=hv, t=t, j=j, st=st, sp=sp: e.matmul(
                            pOP[:, j * 512:(j + 1) * 512],
                            OX[:, hv * S + t * 128:hv * S + (t + 1) * 128],
                            ow_bf[:, hv * D + j * 512:hv * D + (j + 1) * 512],
                            start=st, stop=sp),
                            inc=(tMM if (sp and j == 3) else None))
                op_mm = tMM.n
                if pend_tr is not None:
                    emit_x1_transposes(*pend_tr)
                    pend_tr = None
                P.w("vector", tMM, op_mm)
                P.w("vector", dGs[i], x_ld)
                P.wf("vector", x2_free)
                P.op("vector", lambda e, i=i: e.tensor_add(x2[:], pOP[:], xst4[i][:]), inc=vE)
                x2_done = vE.n
                pop_free = (vE, x2_done)
                x4_free[i] = (vE, x2_done)
                if t + 2 < TC:
                    P.wf("gpsimd", x4_free[i])
                    P.op("gpsimd", lambda e, t2=t + 2, i=i: e.dma_start(
                        out=xst4[i][:], in_=x_hbm[t2 * 128:(t2 + 1) * 128, :]),
                        inc=dGs[i], k=16)
                    x_ld4[t + 2] = dGs[i].n
                P.w("vector", vE, x2_done)
                P.op("vector", lambda e: e.reduce_sum(stats[:, 10:11], x2[:], axis=AX), inc=vStat)
                P.w("vector", vStat, vStat.n)
                P.op("vector", lambda e: e.tensor_scalar_mul(
                    stats[:, 11:12], stats[:, 10:11], -1.0 / D), inc=vStat)
                P.op("vector", lambda e: e.tensor_scalar_mul(
                    stats[:, 14:15], stats[:, 10:11], 1.0 / D), inc=vStat)
                nm_done = vStat.n
                P.w("scalar", sAct, sAct.n)
                P.w("scalar", vStat, nm_done)
                P.op("scalar", lambda e: e.activation(
                    junk4[:], x2[:], AF.Square, bias=stats[:, 11:12],
                    accum_out=stats[:, 12:13]), inc=sAct)
                P.w("vector", sAct, sAct.n)
                P.op("vector", lambda e: e.tensor_scalar(
                    stats[:, 13:14], stats[:, 12:13], 1.0 / D, EPS_LN,
                    op0=ALU.mult, op1=ALU.add), inc=vStat)
                P.w("vector", vStat, vStat.n)
                P.op("vector", lambda e: e.reciprocal(stats[:, 13:14], stats[:, 13:14]), inc=vStat)
                P.w("scalar", vStat, vStat.n)
                P.op("scalar", lambda e: e.activation(
                    stats[:, 13:14], stats[:, 13:14], AF.Sqrt), inc=sAct)
                P.w("vector", sAct, sAct.n)
                P.wf("vector", x1tm_free[i])
                P.op("vector", lambda e, i=i: e.tensor_scalar(
                    x1tm[i][:], x2[:], stats[:, 14:15], stats[:, 13:14],
                    op0=ALU.subtract, op1=ALU.mult), inc=vE)
                x1tm_done = vE.n
                x2_free = (vE, x1tm_done)
                pend_tr = (t, i, x1tm_done)
            emit_x1_transposes(*pend_tr)

            a4_done_v = vE.n

        dump("x1fm", OX, DC * S, BF16, vE, a4_done_v)

        gate()
        with ExitStack() as phF:
            def sbF(name, shape, dt=BF16):
                return phF.enter_context(nc.sbuf_tensor(name, list(shape), dt))

            def psF(name, shape):
                return phF.enter_context(nc.psum_tensor(name, list(shape), F32))

            h_fm = sbF("h_fm", [128, IC * 512])
            gbf = [sbF(f"gbf{i}", [128, D]) for i in range(2)]
            ubf = [sbF(f"ubf{i}", [128, D]) for i in range(2)]
            dbf = [sbF(f"dbf{i}", [128, INTER // 2]) for i in range(2)]
            sig = sbF("sig", [128, 512])
            tmid = sbF("tmid", [128, 512], F32)
            x2f = [sbF(f"x2f{i}", [128, 512], F32) for i in range(2)]
            x2tm = sbF("x2tm", [128, 4 * D], F32)
            outst = [sbF("outst0", [128, D], F32)] * 2
            gates_sb = sbF("gates_sb", [128, IC], F32)
            cIf = sbF("cIf", [128, 128], F32)
            ups_sb = sbF("ups_sb", [128, IC], F32)
            downs_sb = sbF("downs_sb", [128, DC], F32)

            pG = [psF("pG0", [128, 512]), psF("pG1", [128, 512])]
            pU = [psF("pU0", [128, 512]), psF("pU1", [128, 512])]
            pD = [psF("pD0", [128, 512]), psF("pD1", [128, 512])]
            pT2 = psF("pT2", [128, 512])
            lc_gp = mk_path(sbF, "gp", "gpsimd", "vector")
            lc_sy = mk_path(sbF, "sy2", "sync", "scalar", nslots=3)

            sl, ld = stage_load(ident_in[:, 0:128], 128)
            P.w("vector", dWs[sl], ld)
            P.op("vector", lambda e, i=sl: e.tensor_copy(cIf[:], stage[i][:, 0:128]), inc=cW)
            stage_free[sl] = (cW, cW.n)
            cif_ld = cW.n
            P.op("sync", lambda e: e.dma_start(out=gates_sb[:], in_=gates_c), inc=dCF, k=16)
            P.op("sync", lambda e: e.dma_start(out=ups_sb[:], in_=ups_c), inc=dCF, k=16)
            P.op("sync", lambda e: e.dma_start(out=downs_sb[:], in_=downs_c), inc=dCF, k=16)
            scales_ld = dCF.n

            gbf_free = [None, None]
            ubf_free = [None, None]
            dbf_free = [None, None]
            pg_free = [None, None]
            pu_free = [None, None]
            pd_free = [None, None]
            pt2_free = None
            x2f_free = [None, None]
            h_free = None
            sig_free = tmid_free = None
            x2tm_free = None
            out_free = [None, None]
            HALF = IC // 2

            for grp in range(NGRP):
                gc0 = grp * 512
                def gu_loads(ii):
                    wi = ii % 2
                    P.wf("scalar", gbf_free[wi])
                    g_ready = lc_sy(gate_l[ii], gbf[wi], 0, D)
                    P.wf("vector", ubf_free[wi])
                    u_ready = lc_gp(up_l[ii], ubf[wi], 0, D)
                    return g_ready, u_ready

                pre = gu_loads(0)
                P.w("tensor", vE, a4_done_v)
                P.wf("tensor", h_free)
                for ii in range(IC):
                    wi = ii % 2
                    g_ready, u_ready = pre
                    P.wf("tensor", g_ready)
                    P.wf("tensor", pg_free[wi])
                    for d in range(DC):
                        P.op("tensor", lambda e, wi=wi, d=d, gc0=gc0: e.matmul(
                            pG[wi][:], gbf[wi][:, d * 128:(d + 1) * 128],
                            OX[:, d * S + gc0:d * S + gc0 + 512],
                            start=(d == 0), stop=(d == DC - 1)),
                            inc=(tMM if d == DC - 1 else None))
                    g_mm = tMM.n
                    P.wf("tensor", u_ready)
                    P.wf("tensor", pu_free[wi])
                    for d in range(DC):
                        P.op("tensor", lambda e, wi=wi, d=d, gc0=gc0: e.matmul(
                            pU[wi][:], ubf[wi][:, d * 128:(d + 1) * 128],
                            OX[:, d * S + gc0:d * S + gc0 + 512],
                            start=(d == 0), stop=(d == DC - 1)),
                            inc=(tMM if d == DC - 1 else None))
                    u_mm = tMM.n
                    gbf_free[wi] = (tMM, g_mm)
                    ubf_free[wi] = (tMM, u_mm)
                    if ii + 1 < IC:
                        pre = gu_loads(ii + 1)
                    P.w("scalar", tMM, g_mm)
                    P.w("scalar", dCF, scales_ld)
                    P.wf("scalar", sig_free)
                    P.op("scalar", lambda e, wi=wi, ii=ii: e.activation(
                        sig[:], pG[wi][:], AF.Sigmoid, scale=gates_sb[:, ii:ii + 1]), inc=sAct)
                    sig_done = sAct.n
                    P.w("vector", sAct, sig_done)
                    P.wf("vector", tmid_free)
                    P.op("vector", lambda e, wi=wi, ii=ii: e.scalar_tensor_tensor(
                        tmid[:], pG[wi][:], gates_sb[:, ii:ii + 1], sig[:],
                        op0=ALU.mult, op1=ALU.mult), inc=vE)
                    tm_done = vE.n
                    pg_free[wi] = (vE, tm_done)
                    sig_free = (vE, tm_done)
                    P.w("vector", tMM, u_mm)
                    P.w("vector", vE, tm_done)
                    P.op("vector", lambda e, wi=wi, ii=ii: e.scalar_tensor_tensor(
                        h_fm[:, ii * 512:(ii + 1) * 512], pU[wi][:], ups_sb[:, ii:ii + 1], tmid[:],
                        op0=ALU.mult, op1=ALU.mult), inc=vE)
                    h_done = vE.n
                    pu_free[wi] = (vE, h_done)
                    tmid_free = (vE, h_done)
                h_ready = vE.n
                def down_loads(do):
                    P.wf("scalar", dbf_free[0])
                    h0 = lc_sy(down_c[do][:, 0:INTER // 2], dbf[0], 0, INTER // 2)
                    P.wf("vector", dbf_free[1])
                    h1 = lc_gp(down_c[do][:, INTER // 2:INTER], dbf[1], 0, INTER // 2)
                    return [h0, h1]

                pre_d = down_loads(0)
                pend_tr2 = None

                def emit_down_tr(pdo, pxi, pdone):
                    nonlocal pt2_free, x2f_free
                    P.w("tensor", cW, cif_ld)
                    P.w("tensor", vE, pdone)
                    P.wf("tensor", pt2_free)
                    for tc in range(4):
                        P.op("tensor", lambda e, tc=tc, pxi=pxi: e.matmul(
                            pT2[:, tc * 128:(tc + 1) * 128],
                            x2f[pxi][:, tc * 128:(tc + 1) * 128], cIf[:],
                            start=True, stop=True), inc=(tTR if tc == 3 else None))
                    x2f_free[pxi] = (tTR, tTR.n)
                    P.w("vector", tTR, tTR.n)
                    if pdo == 0:
                        P.wf("vector", x2tm_free)
                    oa = bass.AP(x2tm, pdo * 128, [[4 * D, 128], [D, 4], [1, 128]])
                    P.op("vector", lambda e, oa=oa: e.tensor_copy(
                        oa, pT2[:].rearrange("p (a b) -> p a b", a=4)), inc=vE)
                    pt2_free = (vE, vE.n)

                for do in range(DC):
                    di = do % 2
                    half_ready = pre_d
                    P.wf("tensor", pd_free[di])
                    for hh in range(2):
                        P.wf("tensor", half_ready[hh])
                        P.w("tensor", vE, h_ready)
                        for jj in range(HALF):
                            ii = hh * HALF + jj
                            P.op("tensor", lambda e, di=di, hh=hh, jj=jj, ii=ii: e.matmul(
                                pD[di][:], dbf[hh][:, jj * 128:(jj + 1) * 128],
                                h_fm[:, ii * 512:(ii + 1) * 512],
                                start=(ii == 0), stop=(ii == IC - 1)),
                                inc=(tMM if jj == HALF - 1 else None))
                        dbf_free[hh] = (tMM, tMM.n)
                    d_mm = tMM.n
                    if do + 1 < DC:
                        pre_d = down_loads(do + 1)
                    P.w("vector", tMM, d_mm)
                    P.w("vector", dCF, scales_ld)
                    P.wf("vector", x2f_free[di])
                    P.op("vector", lambda e, di=di, do=do, gc0=gc0: e.scalar_tensor_tensor(
                        x2f[di][:], pD[di][:], downs_sb[:, do:do + 1],
                        OX[:, do * S + gc0:do * S + gc0 + 512],
                        op0=ALU.mult, op1=ALU.add), inc=vE)
                    x2f_done = vE.n
                    pd_free[di] = (vE, x2f_done)
                    if pend_tr2 is not None:
                        emit_down_tr(*pend_tr2)
                    pend_tr2 = (do, di, x2f_done)
                emit_down_tr(*pend_tr2)
                x2tm_ready = vE.n
                h_free = (tMM, d_mm)
                for tc in range(4):
                    t_abs = grp * 4 + tc
                    P.w("vector", vE, x2tm_ready)
                    P.op("vector", lambda e, tc=tc: e.reduce_sum(
                        stats[:, 10:11], x2tm[:, tc * D:(tc + 1) * D], axis=AX), inc=vStat)
                    P.w("vector", vStat, vStat.n)
                    P.op("vector", lambda e: e.tensor_scalar_mul(
                        stats[:, 11:12], stats[:, 10:11], -1.0 / D), inc=vStat)
                    P.op("vector", lambda e: e.tensor_scalar_mul(
                        stats[:, 14:15], stats[:, 10:11], 1.0 / D), inc=vStat)
                    nm_done = vStat.n
                    P.w("scalar", sAct, sAct.n)
                    P.w("scalar", vStat, nm_done)
                    P.wf("scalar", out_free[tc % 2])
                    P.op("scalar", lambda e, tc=tc: e.activation(
                        outst[tc % 2][:], x2tm[:, tc * D:(tc + 1) * D], AF.Square,
                        bias=stats[:, 11:12], accum_out=stats[:, 12:13]), inc=sAct)
                    P.w("vector", sAct, sAct.n)
                    P.op("vector", lambda e: e.tensor_scalar(
                        stats[:, 13:14], stats[:, 12:13], 1.0 / D, EPS_LN,
                        op0=ALU.mult, op1=ALU.add), inc=vStat)
                    P.w("vector", vStat, vStat.n)
                    P.op("vector", lambda e: e.reciprocal(stats[:, 13:14], stats[:, 13:14]), inc=vStat)
                    P.w("scalar", vStat, vStat.n)
                    P.op("scalar", lambda e: e.activation(
                        stats[:, 13:14], stats[:, 13:14], AF.Sqrt), inc=sAct)
                    P.w("vector", sAct, sAct.n)
                    P.op("vector", lambda e, tc=tc: e.tensor_scalar(
                        outst[tc % 2][:], x2tm[:, tc * D:(tc + 1) * D],
                        stats[:, 14:15], stats[:, 13:14],
                        op0=ALU.subtract, op1=ALU.mult), inc=vE)
                    apply_done = vE.n
                    P.w("gpsimd", vE, apply_done)
                    P.op("gpsimd", lambda e, t_abs=t_abs, tc=tc: e.dma_start(
                        out=out_hbm[t_abs * 128:(t_abs + 1) * 128, :], in_=outst[tc % 2][:]),
                        inc=dOut, k=16)
                    out_free[0] = out_free[1] = (dOut, dOut.n)
                x2tm_free = (vE, vE.n)

            P.w("gpsimd", dOut, dOut.n)
            P.w("gpsimd", dDump, dDump.n)

        oxStack.close()
        P.emit()
    return nc


def _host_prep(inputs, S=2048):
    ac = np.ascontiguousarray
    qb = inputs["L0_attn_q_b_w"]
    kvb = inputs["L0_attn_kv_b_w"]
    gate_w = inputs["L0_ffn_gate_w"]
    up_w = inputs["L0_ffn_up_w"]
    down_w = inputs["L0_ffn_down_w"]
    o_w = inputs["L0_attn_o_w"]

    d = {}
    d["qa_wT"] = ac(inputs["L0_attn_q_a_w"].T)
    d["kva_wT"] = ac(inputs["L0_attn_kv_a_w"].T)
    nope_rows = np.concatenate([qb[h * QHD:h * QHD + NOPE] for h in range(H)], 0)
    pe_order = [h for p in range(H // 2) for h in (p, p + H // 2)]
    pe_rows = np.concatenate([qb[h * QHD + NOPE:(h + 1) * QHD] for h in pe_order], 0)
    d["qbn_g"] = ac(nope_rows.T.reshape(QLC, 128, 4, 512).transpose(2, 0, 1, 3))
    d["qbp_g"] = ac(pe_rows.T.reshape(QLC, 128, 2, 512).transpose(2, 0, 1, 3))
    kk = np.stack([kvb[h * 256:h * 256 + 128] for h in range(H)], 0)
    vv = np.stack([kvb[h * 256 + 128:(h + 1) * 256] for h in range(H)], 0)
    d["kvbk_l"] = ac(kk.transpose(0, 2, 1).reshape(H, LATC, 128, 128)
                     .transpose(0, 2, 1, 3).reshape(H, 128, LAT))
    d["kvbv_l"] = ac(vv.transpose(0, 2, 1).reshape(H, LATC, 128, 128)
                     .transpose(0, 2, 1, 3).reshape(H, 128, LAT))
    d["ow_l"] = ac(o_w.T.reshape(DC, 128, D))
    d["gate_l"] = ac(gate_w.reshape(IC, 128, DC, 128).transpose(0, 3, 2, 1)
                     .reshape(IC, 128, D))
    d["up_l"] = ac(up_w.reshape(IC, 128, DC, 128).transpose(0, 3, 2, 1)
                   .reshape(IC, 128, D))
    d["down_c"] = ac(down_w.reshape(DC, 128, IC, 128).transpose(0, 3, 2, 1)
                     .reshape(DC, 128, INTER))
    d["qaln_cols"] = ac(inputs["L0_attn_q_a_ln_w"].reshape(QLC, 128).T)
    d["kvln_cols"] = ac(inputs["L0_attn_kv_a_ln_w"].reshape(LATC, 128).T)
    d["gates_c"] = ac(inputs["L0_ffn_gate_s"].reshape(IC, 128).T)
    d["ups_c"] = ac(inputs["L0_ffn_up_s"].reshape(IC, 128).T)
    d["downs_c"] = ac(inputs["L0_ffn_down_s"].reshape(DC, 128).T)
    d["ident"] = np.eye(128, dtype=np.float32)
    tri = np.tril(np.ones((128, 128), dtype=bool))
    d["mask"] = np.where(tri, 0.0, NEG).astype(np.float32)
    r64 = np.zeros((64, 64), dtype=np.float32)
    for i in range(32):
        r64[i, i + 32] = -1.0
        r64[i + 32, i] = 1.0
    r2 = np.zeros((128, 128), dtype=np.float32)
    r2[0:64, 0:64] = r64
    r2[64:128, 64:128] = r64
    d["rotT"] = ac(r2.T)
    inv_freq = (np.float32(1.0) /
                (np.float32(10000.0) **
                 (np.arange(0, ROPE, 2, dtype=np.float32) / np.float32(ROPE))))
    tpos = np.arange(S, dtype=np.float32)
    freqs = np.outer(tpos, inv_freq).astype(np.float32)
    emb = np.concatenate([freqs, freqs], -1)
    cos_t = np.cos(emb, dtype=np.float32)
    sin_t = np.sin(emb, dtype=np.float32)
    d["cos_tm"] = ac(cos_t)
    d["sins_tm"] = ac(np.concatenate([-sin_t[:, :32], sin_t[:, 32:]], 1))
    d["cos_fm2"] = ac(np.vstack([cos_t.T, cos_t.T]))
    d["sin_fm2"] = ac(np.vstack([sin_t.T, sin_t.T]))
    for k in d:
        d[k] = d[k].astype(np.float32, copy=False)
    return d


def kernel(**inputs):
    x = np.asarray(inputs["L0_attn_hidden_states"])
    B, S, _ = x.shape
    shared = _host_prep(inputs, S=S)
    nc = build_program(S=S)
    in_maps = []
    for b in range(B):
        m = dict(shared)
        m["x"] = np.ascontiguousarray(x[b])
        in_maps.append(m)
    res = run_bass_kernel_spmd(nc, in_maps, core_ids=list(range(B)))
    return np.stack([np.asarray(res.results[b]["out"], dtype=np.float32)
                     for b in range(B)], 0)
